# version 03
# baseline (speedup 1.0000x reference)
import contextlib
import numpy as np

import concourse.bass as bass
import concourse.mybir as mybir
from concourse.bass_utils import run_bass_kernel_spmd

F32 = mybir.dt.float32
F16 = mybir.dt.float16
AF = mybir.ActivationFunctionType
ALU = mybir.AluOpType
AX = mybir.AxisListType

D = 1024
L = 2048
DI = 2048
NS = 16
DTR = 64
NE = 8
DFF = 4096
KC = D // 128
OC = DI // 128
FCN = DFF // 128
NTT = L // 512
NTC = L // 128
NDH = 2
DCH = OC // NDH


def build(debug=False, noscan=False, novecda=False, vecda=False, nobc=False, ytmp_sep=True):
    nc = bass.Bass("TRN2", target_bir_lowering=False)

    x_tm = nc.dram_tensor("x_tm", [L, D], F32, kind="ExternalInput").ap()
    x_fm = nc.dram_tensor("x_fm", [D, L], F32, kind="ExternalInput").ap()
    ipw = nc.dram_tensor("ipw", [D, 2 * DI], F32, kind="ExternalInput").ap()
    cw = nc.dram_tensor("cw", [4, D, DI], F32, kind="ExternalInput").ap()
    cb = nc.dram_tensor("cb", [128, 16], F32, kind="ExternalInput").ap()
    xpw = nc.dram_tensor("xpw", [DI, 96], F32, kind="ExternalInput").ap()
    dtw = nc.dram_tensor("dtw", [DTR, DI], F32, kind="ExternalInput").ap()
    dtb = nc.dram_tensor("dtb", [128, 16], F32, kind="ExternalInput").ap()
    alog = nc.dram_tensor("alog", [DI, NS], F32, kind="ExternalInput").ap()
    dvec = nc.dram_tensor("dvec", [128, 16], F32, kind="ExternalInput").ap()
    opw = nc.dram_tensor("opw", [DI, D], F32, kind="ExternalInput").ap()
    w1t = nc.dram_tensor("w1t", [NE, D, DFF], F32, kind="ExternalInput").ap()
    w2t = nc.dram_tensor("w2t", [NE, DFF, D], F32, kind="ExternalInput").ap()
    wrt = nc.dram_tensor("wrt", [D, NE], F32, kind="ExternalInput").ap()
    out = nc.dram_tensor("out", [L, D], F32, kind="ExternalOutput").ap()

    sc_delta = nc.dram_tensor("sc_delta", [OC, 128, L], F16, kind="Internal").ap()
    sc_xc = nc.dram_tensor("sc_xc", [OC, 128, L], F16, kind="Internal").ap()
    sc_xn = nc.dram_tensor("sc_xn", [128, KC, L], F16, kind="Internal").ap()
    sc_mamba = nc.dram_tensor("sc_mamba", [NTC, 128, D], F32, kind="Internal").ap()
    sc_bc = nc.dram_tensor("sc_bc", [32, L], F16, kind="Internal").ap()

    dbg = {}
    if debug:
        for name, shape, dt in [
            ("d_xn", [128, KC, L], F16), ("d_xc", [OC, 128, L], F16),
            ("d_dbc", [128, L], F32), ("d_gate", [128, 128], F32),
            ("d_mamba", [NTC, 128, D], F32), ("d_delta", [OC, 128, L], F16),
            ("d_ys", [128, OC, L], F16),
        ]:
            dbg[name] = nc.dram_tensor(name, shape, dt, kind="ExternalOutput").ap()

    ops = {k: [] for k in ("tensor", "vector", "scalar", "gpsimd", "sync")}
    last_waited = {k: {} for k in ops}
    es = contextlib.ExitStack()
    sems = {}
    cnt = {}
    for sname in ("SQ", "GQ", "TEN", "VEC", "SCA", "GPC", "WQa", "WQb",
                  "W2a", "W2b", "RQa", "RQb", "XQa", "XQb", "XCa", "XCb"):
        sems[sname] = es.enter_context(nc.semaphore(sname))
        cnt[sname] = 0

    def op(eng, meth, *a, waits=(), inc=None, incv=1, **kw):
        wl = []
        for wv in waits:
            if wv is None:
                continue
            sn, val = wv
            if val is None:
                continue
            if last_waited[eng].get(sn, -1) >= val:
                continue
            last_waited[eng][sn] = val
            wl.append((sems[sn], val))
        if inc is not None:
            cnt[inc] += incv
        semh = sems[inc] if inc is not None else None

        def f(e, meth=meth, a=a, kw=kw, wl=wl, semh=semh, incv=incv):
            for (s, v) in wl:
                e.wait_ge(s, v)
            r = getattr(e, meth)(*a, **kw)
            if semh is not None:
                r.then_inc(semh, incv)
        ops[eng].append(f)
        return (inc, cnt[inc]) if inc is not None else None

    def dma(eng, outap, inap, waits=(), sem=None):
        if sem is None:
            sem = "SQ" if eng == "sync" else "GQ"
        return op(eng, "dma_start", waits=waits, inc=sem, incv=16, out=outap,
                  in_=inap)

    PSUM = es.enter_context(nc.psum_tensor("ps", [128, 8 * 512], F32))
    bank_free = [None] * 8

    def bank(i):
        return PSUM[:, i * 512:(i + 1) * 512]

    def mm(bk_or_ap, lhsT, rhs, start, stop, waits=(), inc_anyway=False):
        o = bank(bk_or_ap) if isinstance(bk_or_ap, int) else bk_or_ap
        return op("tensor", "matmul", o, lhsT, rhs, start=start, stop=stop,
                  waits=waits, inc=("TEN" if (stop or inc_anyway) else None))

    def bc_ap(colap, like):
        b, _ = bass.broadcast_tensor_aps(colap, like)
        return b

    dbc_b = es.enter_context(nc.sbuf_tensor("dbc_b", [128, L], F16))
    gate = es.enter_context(nc.sbuf_tensor("gate", [128, 128], F32))
    A_sb = es.enter_context(nc.sbuf_tensor("A_sb", [128, 256], F32))
    cbb = es.enter_context(nc.sbuf_tensor("cbb", [128, 16], F32))
    dtbb = es.enter_context(nc.sbuf_tensor("dtbb", [128, 16], F32))
    dvb = es.enter_context(nc.sbuf_tensor("dvb", [128, 16], F32))
    onesP = es.enter_context(nc.sbuf_tensor("onesP", [128, 128], F16))
    ones1 = es.enter_context(nc.sbuf_tensor("ones1", [1, 128], F16))
    br0 = es.enter_context(nc.sbuf_tensor("br0", [1, 2, L], F16))
    scol = es.enter_context(nc.sbuf_tensor("scol", [128, 160], F32))
    lgt = es.enter_context(nc.sbuf_tensor("lgt", [128, NTC, 8], F32))
    epsb = es.enter_context(nc.sbuf_tensor("epsb", [128, 1], F32))

    dma("sync", cbb[:], cb)
    dma("sync", dtbb[:], dtb)
    dma("sync", dvb[:], dvec)
    ld_alog = None
    for dc in range(OC):
        ld_alog = dma("sync", A_sb[:, dc * 16:(dc + 1) * 16],
                      alog[dc * 128:(dc + 1) * 128, :])
    ld_small = ld_alog
    a_exp = op("scalar", "activation", A_sb[:], A_sb[:], AF.Exp, bias=0.0,
               scale=1.0, waits=[ld_small], inc="SCA")
    a_neg = op("vector", "tensor_scalar_mul", A_sb[:], A_sb[:], -1.0,
               waits=[a_exp], inc="VEC")
    op("vector", "memset", epsb[:], 1e-5)
    op("vector", "memset", onesP[:], 1.0 / 1024.0)
    ones_done = op("vector", "memset", ones1[:], 1.0, inc="VEC")

    g_xn = contextlib.ExitStack()
    xn_bf = g_xn.enter_context(nc.sbuf_tensor("xn_bf", [128, KC, L], F16))

    g1 = contextlib.ExitStack()
    xstr = g1.enter_context(nc.sbuf_tensor("xstr", [128, 2, KC, 512], F32))
    xbb = g1.enter_context(nc.sbuf_tensor("xbb", [128, KC, 512], F16))
    x2b = g1.enter_context(nc.sbuf_tensor("x2b", [128, KC, 512], F16))
    meanb = g1.enter_context(nc.sbuf_tensor("meanb", [128, 512], F32))
    varb = g1.enter_context(nc.sbuf_tensor("varb", [128, 512], F32))
    stdb = g1.enter_context(nc.sbuf_tensor("stdb", [128, 512], F32))
    rstdb = g1.enter_context(nc.sbuf_tensor("rstdb", [128, 512], F32))
    tmp2 = g1.enter_context(nc.sbuf_tensor("tmp2", [128, 3, 512], F32))
    xnf = g1.enter_context(nc.sbuf_tensor("xnf", [128, KC, 512], F32))
    wrl = g1.enter_context(nc.sbuf_tensor("wrl", [128, KC, 8], F32))

    wr_ld = dma("sync", wrl[:], wrt.rearrange("(j p) e -> p j e", p=128))
    x_fm_r = x_fm.rearrange("(c p) l -> p c l", p=128)
    lg_done = [None] * NTC
    xstr_rd = [None, None]
    xstr_fr = [None, None]
    xstr_fr_s = [None, None]
    mm_mean_done = None
    mm_msq_done = None
    xn_done = None
    for tt in range(NTT):
        s = tt % 2
        xstr_rd[s] = dma("sync", xstr[:, s], x_fm_r[:, :, tt * 512:(tt + 1) * 512],
                         waits=[xstr_fr[s], xstr_fr_s[s],
                                wr_ld if tt == 0 else None],
                         sem=("XQa", "XQb")[s])
        cv = op("vector", "tensor_copy", xbb[:], xstr[:, s],
                waits=[xstr_rd[s], mm_mean_done], inc="VEC")
        sq = op("scalar", "activation", x2b[:], xstr[:, s], AF.Square, bias=0.0,
                scale=1.0, waits=[xstr_rd[s], mm_msq_done], inc="SCA")
        xstr_fr_s[s] = sq
        for kc in range(KC):
            mm_mean_done = mm(0, onesP[:], xbb[:, kc, :], kc == 0, kc == KC - 1,
                              waits=[cv, ones_done, bank_free[0]])
        for kc in range(KC):
            mm_msq_done = mm(1, onesP[:], x2b[:, kc, :], kc == 0, kc == KC - 1,
                             waits=[sq, bank_free[1]])
        mcv = op("vector", "tensor_copy", meanb[:], bank(0), waits=[mm_mean_done],
                 inc="VEC")
        bank_free[0] = mcv
        op("vector", "tensor_copy", varb[:], bank(1), waits=[mm_msq_done])
        va = op("vector", "scalar_tensor_tensor", tmp2[:, 2, :], meanb[:], -1.0,
                meanb[:], ALU.mult, ALU.mult, inc="VEC")
        bank_free[1] = va
        va2 = op("vector", "tensor_add", varb[:], varb[:], tmp2[:, 2, :],
                 inc="VEC")
        sqr = op("scalar", "activation", stdb[:], varb[:], AF.Sqrt,
                 bias=epsb[:, 0:1], scale=1.0, waits=[va2], inc="SCA")
        op("vector", "reciprocal", rstdb[:], stdb[:], waits=[sqr])
        xnf_done = None
        for k in range(KC + 2):
            if k < KC:
                op("vector", "tensor_sub", tmp2[:, k % 2, :], xstr[:, s, k, :],
                   meanb[:])
            if 1 <= k < KC + 1:
                xnf_done = op("vector", "tensor_mul", xnf[:, k - 1, :],
                              tmp2[:, (k - 1) % 2, :], rstdb[:], inc="VEC")
            if k >= 2:
                xn_done = op("vector", "tensor_copy",
                             xn_bf[:, k - 2, tt * 512:(tt + 1) * 512],
                             xnf[:, k - 2, :], inc="VEC")
        xstr_fr[s] = xn_done
        for tcl in range(4):
            tc = tt * 4 + tcl
            bk = 2 + (tcl % 2)
            for kc in range(KC):
                t = mm(bank(bk)[:, 0:8], xnf[:, kc, tcl * 128:(tcl + 1) * 128],
                       wrl[:, kc, :], kc == 0, kc == KC - 1,
                       waits=[xnf_done, wr_ld, bank_free[bk]])
            lg_done[tc] = op("vector", "tensor_copy", lgt[:, tc, :],
                             bank(bk)[:, 0:8], waits=[t], inc="VEC")
            bank_free[bk] = lg_done[tc]
    g1.close()

    g8 = contextlib.ExitStack()
    pbuf = g8.enter_context(nc.sbuf_tensor("pbuf", [128, NTC, 8], F32))
    prbuf = g8.enter_context(nc.sbuf_tensor("prbuf", [128, NTC, 8], F32))
    mk1b = g8.enter_context(nc.sbuf_tensor("mk1b", [128, NTC, 8], F32))
    mk2b = g8.enter_context(nc.sbuf_tensor("mk2b", [128, NTC, 8], F32))
    pmb = g8.enter_context(nc.sbuf_tensor("pmb", [128, NTC, 8], F32))

    def C(i):
        return scol[:, i:i + 1]

    def col(b, tc):
        return C(8 + b * NTC + tc)
    for tc in range(NTC):
        op("vector", "reduce_max", col(0, tc), lgt[:, tc, :], axis=AX.X,
           negate=True, waits=[lg_done[tc]])
    for tc in range(NTC):
        op("vector", "scalar_tensor_tensor", pbuf[:, tc, :], lgt[:, tc, :], 1.0,
           bc_ap(col(0, tc), lgt[:, tc, :]), ALU.mult, ALU.add)
    nm_mark = op("vector", "tensor_copy", C(0), C(0), inc="VEC")
    exp_done = None
    for tc in range(NTC):
        exp_done = op("scalar", "activation", pbuf[:, tc, :], pbuf[:, tc, :],
                      AF.Exp, bias=0.0, scale=1.0, waits=[nm_mark], inc="SCA")
    exp_mark = op("scalar", "copy", C(1), C(1), inc="SCA")
    for tc in range(NTC):
        op("vector", "reduce_sum", col(1, tc), pbuf[:, tc, :], axis=AX.X,
           waits=[exp_mark])
    for tc in range(NTC):
        op("vector", "reciprocal", col(2, tc), col(1, tc))
    for tc in range(NTC):
        op("vector", "tensor_mul", prbuf[:, tc, :], pbuf[:, tc, :],
           bc_ap(col(2, tc), pbuf[:, tc, :]))
    for tc in range(NTC):
        op("vector", "reduce_max", col(3, tc), prbuf[:, tc, :], axis=AX.X)
    for tc in range(NTC):
        op("vector", "tensor_tensor", mk1b[:, tc, :], prbuf[:, tc, :],
           bc_ap(col(3, tc), prbuf[:, tc, :]), ALU.is_ge)
    for tc in range(NTC):
        op("vector", "scalar_tensor_tensor", pmb[:, tc, :], mk1b[:, tc, :],
           -2.0, prbuf[:, tc, :], ALU.mult, ALU.add)
    for tc in range(NTC):
        op("vector", "reduce_max", col(4, tc), pmb[:, tc, :], axis=AX.X)
    for tc in range(NTC):
        op("vector", "tensor_tensor", mk2b[:, tc, :], pmb[:, tc, :],
           bc_ap(col(4, tc), pmb[:, tc, :]), ALU.is_ge)
    for tc in range(NTC):
        op("vector", "tensor_add", col(5, tc), col(3, tc), col(4, tc))
    for tc in range(NTC):
        op("vector", "reciprocal", col(6, tc), col(5, tc))
    for tc in range(NTC):
        op("vector", "tensor_mul", col(7, tc), col(3, tc), col(6, tc))
    for tc in range(NTC):
        op("vector", "tensor_mul", col(8, tc), col(4, tc), col(6, tc))
    for tc in range(NTC):
        op("vector", "tensor_mul", pbuf[:, tc, :], mk1b[:, tc, :],
           bc_ap(col(7, tc), mk1b[:, tc, :]))
    for tc in range(NTC):
        op("vector", "tensor_mul", pmb[:, tc, :], mk2b[:, tc, :],
           bc_ap(col(8, tc), mk2b[:, tc, :]))
    gate_done = None
    for tc in range(NTC):
        gate_done = op("vector", "tensor_add", gate[:, tc * 8:(tc + 1) * 8],
                       pbuf[:, tc, :], pmb[:, tc, :], inc="VEC")
    g8.close()
    if debug:
        dma("gpsimd", dbg["d_gate"], gate[:], waits=[gate_done])
        dma("gpsimd", dbg["d_xn"], xn_bf[:], waits=[xn_done])

    g2 = contextlib.ExitStack()
    dbc_f = g2.enter_context(nc.sbuf_tensor("dbc_f", [128, L], F32))
    xc_bf = g2.enter_context(nc.sbuf_tensor("xc_bf", [128, OC, L], F16))
    g2b = contextlib.ExitStack()
    xi_bf = g2b.enter_context(nc.sbuf_tensor("xi_bf", [128, 8, 3 + L], F16))
    wland = g2b.enter_context(nc.sbuf_tensor("wland", [128, 1, KC, 512], F32))
    cvtb = g2b.enter_context(nc.sbuf_tensor("cvtb", [128, 4, KC, 512], F16))

    ipw_r = ipw.rearrange("(j p) f -> p j f", p=128)
    land_fr = None
    cvt_fr = [None] * 4
    last_xc_ev = None

    for g in range(2):
        pad_done = None
        for j in range(8):
            pad_done = op("vector", "memset", xi_bf[:, j, 0:3], 0.0, inc="VEC")
        xi_done = None
        for cb_i in range(2):
            c0 = g * 1024 + cb_i * 512
            s = cb_i % 2
            r = dma("sync", wland[:, 0], ipw_r[:, :, c0:c0 + 512],
                    waits=[land_fr, gate_done, exp_mark])
            wcv = op("vector", "tensor_copy", cvtb[:, s], wland[:, 0],
                     waits=[r, cvt_fr[s]], inc="VEC")
            land_fr = wcv
            for ocb in range(4):
                oc_local = cb_i * 4 + ocb
                for tt in range(NTT):
                    bk = 4 + ((ocb * NTT + tt) % 4)
                    for j in range(KC):
                        t = mm(bk, cvtb[:, s, j, ocb * 128:(ocb + 1) * 128],
                               xn_bf[:, j, tt * 512:(tt + 1) * 512],
                               j == 0, j == KC - 1,
                               waits=[wcv, xn_done, pad_done, bank_free[bk]])
                    xi_done = op("scalar", "copy",
                                 xi_bf[:, oc_local, 3 + tt * 512:3 + (tt + 1) * 512],
                                 bank(bk), waits=[t], inc="SCA")
                    bank_free[bk] = xi_done
            cvt_fr[s] = t
        for ob in range(2):
            o0 = g * 1024 + ob * 512
            wcvs = []
            for w in range(4):
                r = dma("sync", wland[:, 0],
                        cw[w].rearrange("(j p) f -> p j f", p=128)[:, :, o0:o0 + 512],
                        waits=[land_fr])
                wcv = op("vector", "tensor_copy", cvtb[:, w], wland[:, 0],
                         waits=[r, cvt_fr[w]], inc="VEC")
                land_fr = wcv
                wcvs.append(wcv)
            for ocb in range(4):
                oc_g = g * 8 + ob * 4 + ocb
                for tt in range(NTT):
                    bk = 4 + ((ocb * NTT + tt) % 4)
                    first = True
                    for w in range(4):
                        for j in range(8):
                            t = mm(bk, cvtb[:, w, j, ocb * 128:(ocb + 1) * 128],
                                   xi_bf[:, j, w + tt * 512:w + (tt + 1) * 512],
                                   first, (w == 3 and j == 7),
                                   waits=[wcvs[w], xi_done, bank_free[bk]])
                            first = False
                    last_xc_ev = op("scalar", "activation",
                                    xc_bf[:, oc_g, tt * 512:(tt + 1) * 512],
                                    bank(bk), AF.Silu,
                                    bias=cbb[:, oc_g:oc_g + 1], scale=1.0,
                                    waits=[t, ld_small], inc="SCA")
                    bank_free[bk] = last_xc_ev
            for w in range(4):
                cvt_fr[w] = t
    xc_done = last_xc_ev
    p23_ten = ("TEN", cnt["TEN"])
    p23_vec = ("VEC", cnt["VEC"])
    g2b.close()

    xpl = g2.enter_context(nc.sbuf_tensor("xpl", [128, OC, 96], F32))
    xpb = g2.enter_context(nc.sbuf_tensor("xpb", [128, OC, 96], F16))
    r = dma("sync", xpl[:], xpw.rearrange("(j p) f -> p j f", p=128),
            waits=[p23_ten, p23_vec])
    xp_cv = op("vector", "tensor_copy", xpb[:], xpl[:], waits=[r], inc="VEC")
    for tt in range(NTT):
        bk = tt % 2
        for j in range(OC):
            t = mm(bank(bk)[0:96, :], xpb[:, j, :],
                   xc_bf[:, j, tt * 512:(tt + 1) * 512], j == 0, j == OC - 1,
                   waits=[xp_cv, xc_done, bank_free[bk]])
        ev = op("vector", "tensor_copy", dbc_f[0:96, tt * 512:(tt + 1) * 512],
                bank(bk)[0:96, :], waits=[t], inc="VEC")
        bank_free[bk] = ev
    dbc_cv = op("vector", "tensor_copy", dbc_b[0:96, :], dbc_f[0:96, :],
                inc="VEC")
    dbc_guard = op("vector", "tensor_copy", dbc_b[0:64, 0:512],
                   dbc_b[0:64, 0:512], inc="VEC")
    bc_rows = dma("gpsimd", sc_bc, dbc_b[64:96, :], waits=[dbc_guard])
    if debug:
        dma("gpsimd", dbg["d_dbc"], dbc_f[:], waits=[dbc_cv])

    dtl = g2.enter_context(nc.sbuf_tensor("dtl", [64, DI], F32))
    dtwb = g2.enter_context(nc.sbuf_tensor("dtwb", [64, DI], F16))
    sp1 = g2.enter_context(nc.sbuf_tensor("sp1", [128, 512], F32))
    sp2 = g2.enter_context(nc.sbuf_tensor("sp2", [128, 512], F32))
    dstage = g2.enter_context(nc.sbuf_tensor("dstage", [128, 3, 512], F16))
    r = dma("sync", dtl[:], dtw, waits=[xp_cv])
    dt_cv = op("vector", "tensor_copy", dtwb[:], dtl[:], waits=[r], inc="VEC")
    d_spill = [[None] * NTT for _ in range(OC)]
    xc_spill = [None] * OC
    dst_fr = [None, None, None]
    sp_fr = None
    pend = None
    for ocg in range(OC):
        for tt in range(NTT):
            bk = 2 + ((ocg * NTT + tt) % 2)
            t = mm(bk, dtwb[:, ocg * 128:(ocg + 1) * 128],
                   dbc_b[0:64, tt * 512:(tt + 1) * 512], True, True,
                   waits=[dt_cv, dbc_cv, bank_free[bk]])
            bc = dtbb[:, ocg:ocg + 1]
            op("scalar", "activation", sp1[:], bank(bk), AF.Abs, bias=bc,
               scale=1.0, waits=[t, sp_fr])
            op("scalar", "activation", sp1[:], sp1[:], AF.Exp, bias=0.0,
               scale=-1.0)
            op("scalar", "activation", sp1[:], sp1[:], AF.Ln, bias=1.0, scale=1.0)
            ab = op("scalar", "activation", sp2[:], bank(bk), AF.Relu, bias=bc,
                    scale=1.0, inc="SCA")
            bank_free[bk] = ab
            sl = (ocg * NTT + tt) % 3
            dv = op("vector", "tensor_add", dstage[:, sl, :], sp1[:], sp2[:],
                    waits=[ab, dst_fr[sl]], inc="VEC")
            sp_fr = dv
            if pend is not None:
                po, pt, psl = pend
                d_spill[po][pt] = dma("gpsimd",
                                      sc_delta[po][:, pt * 512:(pt + 1) * 512],
                                      dstage[:, psl, :], waits=[dv])
                dst_fr[psl] = d_spill[po][pt]
            pend = (ocg, tt, sl)
        xc_spill[ocg] = dma("gpsimd", sc_xc[ocg], xc_bf[:, ocg, :],
                            waits=[xc_done, ("SCA", cnt["SCA"])])
    guard = op("vector", "tensor_copy", sp1[:], sp1[:], inc="VEC")
    po, pt, psl = pend
    d_spill[po][pt] = dma("gpsimd", sc_delta[po][:, pt * 512:(pt + 1) * 512],
                          dstage[:, psl, :], waits=[guard])
    xn_spill = dma("gpsimd", sc_xn, xn_bf[:], waits=[xn_done, guard])
    gq_spills = ("GQ", cnt["GQ"])
    p45_ten = ("TEN", cnt["TEN"])
    p45_vec = ("VEC", cnt["VEC"])
    p45_sca = ("SCA", cnt["SCA"])
    g2.close()
    g_xn.close()
    if debug:
        dma("gpsimd", dbg["d_delta"], sc_delta, waits=[gq_spills])
        dma("gpsimd", dbg["d_xc"], sc_xc, waits=[gq_spills])

    g_yac = contextlib.ExitStack()
    yac = g_yac.enter_context(nc.sbuf_tensor("yac", [128, OC, L], F16))
    g3 = contextlib.ExitStack()
    bbn = g3.enter_context(nc.sbuf_tensor("bbn", [128, 2, L], F16))
    ccn = g3.enter_context(nc.sbuf_tensor("ccn", [128, 2, L], F16))
    dAb = g3.enter_context(nc.sbuf_tensor("dAb", [128, 3, L], F16))
    dBub = g3.enter_context(nc.sbuf_tensor("dBub", [128, 2, L], F16))
    hbf = g3.enter_context(nc.sbuf_tensor("hbf", [128, L], F16))
    ytm = g3.enter_context(nc.sbuf_tensor("ytm", [128, 2, L], F16))

    bc_ev = [None, None]
    bc_used = [None, None]
    half_sca = None

    def build_bc(n, slot, extra=()):
        b1, _ = bass.broadcast_tensor_aps(sc_bc[n:n + 1, :], bbn[:, slot, :])
        c1, _ = bass.broadcast_tensor_aps(sc_bc[16 + n:17 + n, :], ccn[:, slot, :])
        dma("gpsimd", bbn[:, slot, :], b1,
            waits=[bc_rows, bc_used[slot]] + list(extra))
        bc_ev[slot] = dma("gpsimd", ccn[:, slot, :], c1, waits=[])

    half_ev = None
    prev_half_gpc = None
    for dh in range(NDH):
        g3h = contextlib.ExitStack()
        dlt = g3h.enter_context(nc.sbuf_tensor(f"dlt{dh}", [128, DCH, L], F16))
        xcs = g3h.enter_context(nc.sbuf_tensor(f"xcs{dh}", [128, DCH, L], F16))
        dub = xcs
        for i in range(DCH):
            dc = dh * DCH + i
            dma("sync", dlt[:, i, :], sc_delta[dc],
                waits=[gq_spills, half_ev, half_sca, p45_vec, p45_sca, p45_ten])
            dma("sync", xcs[:, i, :], sc_xc[dc], waits=[])
        all_loads = ("SQ", cnt["SQ"])
        du_ev = [None] * DCH
        drd = [all_loads] * DCH
        for i in range(DCH):
            dc = dh * DCH + i
            op("vector", "tensor_scalar", yac[:, dc, :], xcs[:, i, :],
               dvb[:, dc:dc + 1], None, ALU.mult, waits=[all_loads])
            du_ev[i] = op("vector", "tensor_mul", xcs[:, i, :], dlt[:, i, :],
                          xcs[:, i, :], inc="VEC")
        if dh == 0:
            build_bc(0, 0, extra=[dbc_cv, ("VEC", cnt["VEC"])])
        else:
            build_bc(0, 0)

        NSL = NS * DCH
        da_ev = [None] * (NSL + 4)
        scan_ev = [None] * NSL
        gp_yadd = [None] * NSL
        gp_guard = None

        def emit_da(k):
            if k >= NSL + 2:
                return
            if k >= NSL:
                da_ev[k] = op("scalar", "copy", dAb[:, k % 3, :],
                              dAb[:, k % 3, :],
                              waits=[scan_ev[k - 3] if k >= 3 else None],
                              inc="SCA")
                return
            n, i_loc = divmod(k, DCH)
            dc = dh * DCH + i_loc
            if vecda:
                da_ev[k] = op("vector", "tensor_scalar_mul", dAb[:, k % 3, :],
                              dlt[:, i_loc, :], 0.1,
                              waits=[drd[i_loc], a_neg,
                                     scan_ev[k - 3] if k >= 3 else None],
                              inc="SCA")
            elif novecda:
                da_ev[k] = op("scalar", "mul", dAb[:, k % 3, :],
                              dlt[:, i_loc, :], 0.1,
                              waits=[drd[i_loc], a_neg,
                                     scan_ev[k - 3] if k >= 3 else None],
                              inc="SCA")
            else:
                da_ev[k] = op("scalar", "activation", dAb[:, k % 3, :],
                              dlt[:, i_loc, :], AF.Exp, bias=0.0,
                              scale=A_sb[:, dc * 16 + n:dc * 16 + n + 1],
                              waits=[drd[i_loc], a_neg,
                                     scan_ev[k - 3] if k >= 3 else None],
                              inc="SCA")

        emit_da(0)
        emit_da(1)
        op("vector", "tensor_mul", dBub[:, 0, :], dub[:, 0, :], bbn[:, 0, :],
           waits=[bc_ev[0], du_ev[DCH - 1]])
        for k in range(NSL):
            n, i_loc = divmod(k, DCH)
            dc = dh * DCH + i_loc
            ns = n % 2
            emit_da(k + 2)
            if i_loc == 1 and n + 1 < NS:
                build_bc(n + 1, (n + 1) % 2)
            if noscan:
                scan_ev[k] = op("vector", "tensor_mul", hbf[:],
                                dAb[:, k % 3, :], dBub[:, k % 2, :],
                                waits=[da_ev[k + 2]], inc="VEC")
            else:
                scan_ev[k] = op("vector", "tensor_tensor_scan", hbf[:],
                                dAb[:, k % 3, :], dBub[:, k % 2, :], 0.0,
                                ALU.mult, ALU.add, waits=[da_ev[k + 2]],
                                inc="VEC")
            if k + 1 < NSL:
                n2, i2 = divmod(k + 1, DCH)
                op("vector", "tensor_mul", dBub[:, (k + 1) % 2, :],
                   dub[:, i2, :], bbn[:, n2 % 2, :],
                   waits=[bc_ev[n2 % 2]])
            op("vector", "tensor_mul", ytm[:, k % 2, :], hbf[:], ccn[:, ns, :],
               waits=[("GPC", gp_yadd[k - 2][1]) if k >= 2 else prev_half_gpc],
               inc="VEC")
            if k >= 1:
                pn, pi = divmod(k - 1, DCH)
                pdc = dh * DCH + pi
                gp_yadd[k - 1] = op("gpsimd", "tensor_add", yac[:, pdc, :],
                                    yac[:, pdc, :], ytm[:, (k - 1) % 2, :],
                                    waits=[("VEC", scan_ev[k][1])], inc="GPC")
            if k == NSL - 1:
                vg = op("vector", "tensor_copy", hbf[:], hbf[:], inc="VEC")
                gp_yadd[k] = op("gpsimd", "tensor_add", yac[:, dc, :],
                                yac[:, dc, :], ytm[:, k % 2, :],
                                waits=[vg], inc="GPC")
                gp_guard = op("gpsimd", "tensor_copy", ytm[:, 0, :],
                              ytm[:, 0, :], inc="GPC")
            if i_loc == 0:
                bc_used[(n + 1) % 2] = ("VEC", cnt["VEC"])
        half_ev = ("VEC", cnt["VEC"])
        half_sca = ("SCA", cnt["SCA"])
        half_gpc = ("GPC", cnt["GPC"])
        prev_half_gpc = half_gpc
        bc_used[0] = bc_used[1] = half_ev
        g3h.close()
    scan_vec = ("VEC", cnt["VEC"])
    scan_sca = ("SCA", cnt["SCA"])
    scan_ten = ("TEN", cnt["TEN"])
    scan_gpc = ("GPC", cnt["GPC"])
    g3.close()

    g5 = contextlib.ExitStack()
    xn2 = g5.enter_context(nc.sbuf_tensor("xn2", [128, KC, L], F16))
    opw_b = g5.enter_context(nc.sbuf_tensor("opw_b", [128, OC, D], F16))
    rland = g5.enter_context(nc.sbuf_tensor("rland", [128, 2, KC, 128], F32))
    rbf = g5.enter_context(nc.sbuf_tensor("rbf", [128, 2, KC, 128], F16))
    ressl = g5.enter_context(nc.sbuf_tensor("ressl", [128, 2, L], F16))
    xch = g5.enter_context(nc.sbuf_tensor("xch", [128, 2, D], F32))

    xn_rd = dma("gpsimd", xn2[:], sc_xn,
                waits=[xn_spill, scan_vec, scan_sca, scan_ten, scan_gpc])
    with nc.sbuf_tensor("oland", [128, 4, D], F32) as oland:
        opw_r = opw.rearrange("(c p) d -> p c d", p=128)
        opcv = None
        for blk in range(4):
            r = dma("sync", oland[:], opw_r[:, blk * 4:(blk + 1) * 4, :],
                    waits=[opcv, scan_vec])
            opcv = op("vector", "tensor_copy", opw_b[:, blk * 4:(blk + 1) * 4, :],
                      oland[:], waits=[r], inc="VEC")
        opw_done = opcv

    res_fr = [None, None]
    res_used = [None, None]
    rsl_ev = [None] * (OC + 1)
    ys_ev = [None] * OC

    def emit_res(dc):
        if dc >= OC:
            if dc == OC:
                rsl_ev[OC] = op("scalar", "copy", ressl[:, (OC - 1) % 2, 0:512],
                                ressl[:, (OC - 1) % 2, 0:512], inc="SCA")
            return
        rs = dc % 2
        rr = dma("sync", rland[:, rs],
                 ipw_r[:, :, DI + dc * 128:DI + (dc + 1) * 128],
                 waits=[res_fr[rs]], sem=("RQa", "RQb")[rs])
        rcv = op("vector", "tensor_copy", rbf[:, rs], rland[:, rs],
                 waits=[rr, res_used[rs]], inc="VEC")
        res_fr[rs] = rcv
        bk = 2 + rs
        for q in range(NTT):
            for j in range(KC):
                t = mm(bk, rbf[:, rs, j, :],
                       xn2[:, j, q * 512:(q + 1) * 512], j == 0, j == KC - 1,
                       waits=[rcv, xn_rd, bank_free[bk]])
            rsl_ev[dc] = op("scalar", "activation",
                            ressl[:, rs, q * 512:(q + 1) * 512], bank(bk),
                            AF.Silu, bias=0.0, scale=1.0, waits=[t], inc="SCA")
            bank_free[bk] = rsl_ev[dc]
        res_used[rs] = t

    emit_res(0)
    for dc in range(OC):
        emit_res(dc + 1)
        ys_ev[dc] = op("vector", "tensor_mul", yac[:, dc, :], yac[:, dc, :],
                       ressl[:, dc % 2, :], waits=[rsl_ev[dc + 1]], inc="VEC")
    ys_done = ys_ev[OC - 1]

    mamba_flush = [None] * NTC
    pend_fl = None
    add_ev = None
    for tc in range(NTC):
        xs = tc % 2
        prev_gq = ("GQ", cnt["GQ"]) if tc >= 2 else None
        xr = dma("sync", xch[:, xs, :], x_tm[tc * 128:(tc + 1) * 128, :],
                 waits=[prev_gq], sem=("XCa", "XCb")[xs])
        for dh2 in range(2):
            bk = 4 + ((tc * 2 + dh2) % 4)
            for ocg in range(OC):
                t = mm(bk, yac[:, ocg, tc * 128:(tc + 1) * 128],
                       opw_b[:, ocg, dh2 * 512:(dh2 + 1) * 512],
                       ocg == 0, ocg == OC - 1,
                       waits=[ys_done, opw_done, bank_free[bk]])
            add_ev = op("vector", "tensor_add",
                        xch[:, xs, dh2 * 512:(dh2 + 1) * 512],
                        xch[:, xs, dh2 * 512:(dh2 + 1) * 512], bank(bk),
                        waits=[t, xr], inc="VEC")
            bank_free[bk] = add_ev
            if pend_fl is not None and dh2 == 0:
                ptc, pxs = pend_fl
                mamba_flush[ptc] = dma("gpsimd", sc_mamba[ptc], xch[:, pxs, :],
                                       waits=[add_ev])
                pend_fl = None
        pend_fl = (tc, xs)
    fin_guard = op("vector", "tensor_copy", xch[:, (NTC - 1) % 2, 0:512],
                   xch[:, (NTC - 1) % 2, 0:512], inc="VEC")
    ptc, pxs = pend_fl
    mamba_flush[ptc] = dma("gpsimd", sc_mamba[ptc], xch[:, pxs, :],
                           waits=[fin_guard])
    p7_ten = ("TEN", cnt["TEN"])
    p7_vec = ("VEC", cnt["VEC"])
    p7_sca = ("SCA", cnt["SCA"])
    if debug:
        dma("gpsimd", dbg["d_mamba"], sc_mamba, waits=[("GQ", cnt["GQ"])])
        dma("gpsimd", dbg["d_ys"], yac[:], waits=[fin_guard])
    g5.close()
    g_yac.close()

    g4 = contextlib.ExitStack()
    xn3 = g4.enter_context(nc.sbuf_tensor("xn3", [128, KC, L], F16))
    h_bf = g4.enter_context(nc.sbuf_tensor("h_bf", [128, FCN, 1024], F16))
    w1b = g4.enter_context(nc.sbuf_tensor("w1b", [128, 2, KC, 512], F16))
    w2dh = g4.enter_context(nc.sbuf_tensor("w2dh", [128, 2, FCN // 2, 512], F16))
    out_h = g4.enter_context(nc.sbuf_tensor("out_h", [128, 8, D], F32))

    xn_rd2 = dma("gpsimd", xn3[:], sc_xn,
                 waits=[p7_ten, p7_vec, p7_sca, ("GQ", cnt["GQ"])])
    w1b_fr = [None, None]
    w2_fr = [None, None]
    w2_rd = [None, None]
    h_free = None
    final_flush = None
    ev = None
    w2_r = w2t.rearrange("e (f p) d -> e p f d", p=128)

    for th in range(2):
        init_r = []
        for tci in range(8):
            tc = th * 8 + tci
            init_r.append(dma("gpsimd", out_h[:, tci, :], sc_mamba[tc],
                              waits=[("GQ", cnt["GQ"]), p7_vec, p7_sca]))
        for e in range(NE):
            w1e = w1t[e].rearrange("(j p) f -> p j f", p=128)
            h_done = None

            def w2q(qi):
                dh2, fh = divmod(qi, 2)
                sl = qi % 2
                w2_rd[sl] = dma("gpsimd", w2dh[:, sl],
                                w2_r[e][:, fh * 16:(fh + 1) * 16,
                                        dh2 * 512:(dh2 + 1) * 512],
                                waits=[w2_fr[sl]], sem=("W2a", "W2b")[sl])
            w2q(0)
            w2q(1)
            for fb in range(8):
                s = fb % 2
                wcv = dma("gpsimd", w1b[:, s],
                          w1e[:, :, fb * 512:(fb + 1) * 512],
                          waits=[w1b_fr[s]], sem=("WQa", "WQb")[s])
                for fcb in range(4):
                    fc = fb * 4 + fcb
                    for tt in range(2):
                        bk = (fcb * 2 + tt) % 4
                        for j in range(KC):
                            t = mm(bk, w1b[:, s, j, fcb * 128:(fcb + 1) * 128],
                                   xn3[:, j, th * 1024 + tt * 512:
                                       th * 1024 + (tt + 1) * 512],
                                   j == 0, j == KC - 1,
                                   waits=[wcv, xn_rd2, h_free, bank_free[bk]])
                        if fcb == 3 and tt == 1:
                            w1b_fr[s] = t
                        h_done = op("scalar", "activation",
                                    h_bf[:, fc, tt * 512:(tt + 1) * 512],
                                    bank(bk), AF.Silu, bias=0.0, scale=1.0,
                                    waits=[t], inc="SCA")
                        bank_free[bk] = h_done

            for dh2 in range(2):
                t_grp = None
                for fc in range(FCN):
                    fh, fl = divmod(fc, 16)
                    qi = dh2 * 2 + fh
                    sl = qi % 2
                    if fl == 0 and 1 <= qi < 3:
                        w2q(qi + 1)
                    for tci in range(8):
                        t_grp = mm(tci, h_bf[:, fc, tci * 128:(tci + 1) * 128],
                                   w2dh[:, sl, fl, :], fc == 0, fc == FCN - 1,
                                   waits=[w2_rd[sl], h_done, bank_free[tci]],
                                   inc_anyway=(tci == 7 and fc % 8 == 7))
                    if fl == 15:
                        w2_fr[sl] = t_grp
                for tci in range(8):
                    tc = th * 8 + tci
                    gcol = gate[:, tc * 8 + e:tc * 8 + e + 1]
                    ev = op("vector", "scalar_tensor_tensor",
                            out_h[:, tci, dh2 * 512:(dh2 + 1) * 512],
                            bank(tci), gcol,
                            out_h[:, tci, dh2 * 512:(dh2 + 1) * 512],
                            ALU.mult, ALU.add,
                            waits=[t_grp, gate_done, init_r[tci]], inc="VEC")
                    bank_free[tci] = ev
                h_free = t_grp

        flg = op("vector", "tensor_copy", h_bf[:, 0, :], h_bf[:, 0, :],
                 waits=[ev], inc="VEC")
        for tci in range(8):
            tc = th * 8 + tci
            final_flush = dma("gpsimd", out[tc * 128:(tc + 1) * 128, :],
                              out_h[:, tci, :], waits=[flg])

    g4.close()

    def fin(e_, s_=sems["GQ"], v=final_flush[1]):
        e_.wait_ge(s_, v)
    ops["gpsimd"].append(fin)

    with nc.Block() as block:
        @block.tensor
        def _(e):
            for f in ops["tensor"]:
                f(e)

        @block.vector
        def _(e):
            for f in ops["vector"]:
                f(e)

        @block.scalar
        def _(e):
            for f in ops["scalar"]:
                f(e)

        @block.gpsimd
        def _(e):
            for f in ops["gpsimd"]:
                f(e)

        @block.sync
        def _(e):
            for f in ops["sync"]:
                f(e)

    es.close()
    return nc


_NC_CACHE = {}


def _get_nc(debug=False):
    if debug not in _NC_CACHE:
        _NC_CACHE[debug] = build(debug)
    return _NC_CACHE[debug]


def make_in_maps(x, ipw, cw, cbv, xpw, dtwv, dtbv, alogv, dvecv, opwv, W1, W2, Wr):
    shared = {
        "ipw": np.ascontiguousarray(ipw, dtype=np.float32),
        "cw": np.ascontiguousarray(cw, dtype=np.float32),
        "cb": np.ascontiguousarray(np.asarray(cbv, np.float32).reshape(16, 128).T),
        "xpw": np.ascontiguousarray(xpw, dtype=np.float32),
        "dtw": np.ascontiguousarray(dtwv, dtype=np.float32),
        "dtb": np.ascontiguousarray(np.asarray(dtbv, np.float32).reshape(16, 128).T),
        "alog": np.ascontiguousarray(alogv, dtype=np.float32),
        "dvec": np.ascontiguousarray(np.asarray(dvecv, np.float32).reshape(16, 128).T),
        "opw": np.ascontiguousarray(opwv, dtype=np.float32),
        "w1t": np.ascontiguousarray(np.asarray(W1, np.float32).transpose(0, 2, 1)),
        "w2t": np.ascontiguousarray(np.asarray(W2, np.float32).transpose(0, 2, 1)),
        "wrt": np.ascontiguousarray(np.asarray(Wr, np.float32).T),
    }
    in_maps = []
    for c in range(x.shape[0]):
        m = dict(shared)
        m["x_tm"] = np.ascontiguousarray(x[c])
        m["x_fm"] = np.ascontiguousarray(x[c].T)
        in_maps.append(m)
    return in_maps


def kernel(L0_attn_x, L0_attn_in_proj_w, L0_attn_conv_w, L0_attn_conv_b,
           L0_attn_x_proj_w, L0_attn_dt_proj_w, L0_attn_dt_proj_b,
           L0_attn_A_log, L0_attn_D, L0_attn_out_proj_w,
           L0_ffn_W1, L0_ffn_W2, L0_ffn_Wr):
    x = np.ascontiguousarray(np.asarray(L0_attn_x, dtype=np.float32))
    in_maps = make_in_maps(x, L0_attn_in_proj_w, L0_attn_conv_w, L0_attn_conv_b,
                           L0_attn_x_proj_w, L0_attn_dt_proj_w, L0_attn_dt_proj_b,
                           L0_attn_A_log, L0_attn_D, L0_attn_out_proj_w,
                           L0_ffn_W1, L0_ffn_W2, L0_ffn_Wr)
    nc = _get_nc(False)
    res = run_bass_kernel_spmd(nc, in_maps, core_ids=list(range(8)))
    return np.stack([res.results[c]["out"] for c in range(8)], axis=0)


# version 04
# speedup vs baseline: 1.1411x; 1.1411x over previous
import contextlib
import numpy as np

import concourse.bass as bass
import concourse.mybir as mybir
from concourse.bass_utils import run_bass_kernel_spmd

F32 = mybir.dt.float32
F16 = mybir.dt.float16
AF = mybir.ActivationFunctionType
ALU = mybir.AluOpType
AX = mybir.AxisListType

D = 1024
L = 2048
DI = 2048
NS = 16
DTR = 64
NE = 8
DFF = 4096
KC = D // 128
OC = DI // 128
FCN = DFF // 128
NTT = L // 512
NTC = L // 128
NDH = 2
DCH = OC // NDH


def build(debug=False, noscan=False, novecda=False, vecda=False, nobc=False, ytmp_sep=True):
    nc = bass.Bass("TRN2", target_bir_lowering=False)

    x_tm = nc.dram_tensor("x_tm", [L, D], F32, kind="ExternalInput").ap()
    x_fm = nc.dram_tensor("x_fm", [D, L], F32, kind="ExternalInput").ap()
    ipw = nc.dram_tensor("ipw", [D, 2 * DI], F32, kind="ExternalInput").ap()
    cw = nc.dram_tensor("cw", [4, D, DI], F32, kind="ExternalInput").ap()
    cb = nc.dram_tensor("cb", [128, 16], F32, kind="ExternalInput").ap()
    xpw = nc.dram_tensor("xpw", [DI, 96], F32, kind="ExternalInput").ap()
    dtw = nc.dram_tensor("dtw", [DTR, DI], F32, kind="ExternalInput").ap()
    dtb = nc.dram_tensor("dtb", [128, 16], F32, kind="ExternalInput").ap()
    alog = nc.dram_tensor("alog", [DI, NS], F32, kind="ExternalInput").ap()
    dvec = nc.dram_tensor("dvec", [128, 16], F32, kind="ExternalInput").ap()
    opw = nc.dram_tensor("opw", [DI, D], F32, kind="ExternalInput").ap()
    w1t = nc.dram_tensor("w1t", [NE, D, DFF], F32, kind="ExternalInput").ap()
    w2t = nc.dram_tensor("w2t", [NE, DFF, D], F32, kind="ExternalInput").ap()
    wrt = nc.dram_tensor("wrt", [D, NE], F32, kind="ExternalInput").ap()
    out = nc.dram_tensor("out", [L, D], F32, kind="ExternalOutput").ap()

    sc_delta = nc.dram_tensor("sc_delta", [OC, 128, L], F16, kind="Internal").ap()
    sc_xc = nc.dram_tensor("sc_xc", [OC, 128, L], F16, kind="Internal").ap()
    sc_xn = nc.dram_tensor("sc_xn", [128, KC, L], F16, kind="Internal").ap()
    sc_mamba = nc.dram_tensor("sc_mamba", [NTC, 128, D], F32, kind="Internal").ap()
    sc_bc = nc.dram_tensor("sc_bc", [32, L], F16, kind="Internal").ap()

    dbg = {}
    if debug:
        for name, shape, dt in [
            ("d_xn", [128, KC, L], F16), ("d_xc", [OC, 128, L], F16),
            ("d_dbc", [128, L], F32), ("d_gate", [128, 128], F32),
            ("d_mamba", [NTC, 128, D], F32), ("d_delta", [OC, 128, L], F16),
            ("d_ys", [128, OC, L], F16),
        ]:
            dbg[name] = nc.dram_tensor(name, shape, dt, kind="ExternalOutput").ap()

    ops = {k: [] for k in ("tensor", "vector", "scalar", "gpsimd", "sync")}
    last_waited = {k: {} for k in ops}
    es = contextlib.ExitStack()
    sems = {}
    cnt = {}
    for sname in ("SQ", "GQ", "TEN", "VEC", "SCA", "GPC", "WQa", "WQb",
                  "W2a", "W2b", "RQa", "RQb", "XQa", "XQb", "XCa", "XCb",
                  "CVa", "CVb", "CVc", "CVd"):
        sems[sname] = es.enter_context(nc.semaphore(sname))
        cnt[sname] = 0

    def op(eng, meth, *a, waits=(), inc=None, incv=1, **kw):
        wl = []
        for wv in waits:
            if wv is None:
                continue
            sn, val = wv
            if val is None:
                continue
            if last_waited[eng].get(sn, -1) >= val:
                continue
            last_waited[eng][sn] = val
            wl.append((sems[sn], val))
        if inc is not None:
            cnt[inc] += incv
        semh = sems[inc] if inc is not None else None

        def f(e, meth=meth, a=a, kw=kw, wl=wl, semh=semh, incv=incv):
            for (s, v) in wl:
                e.wait_ge(s, v)
            r = getattr(e, meth)(*a, **kw)
            if semh is not None:
                r.then_inc(semh, incv)
        ops[eng].append(f)
        return (inc, cnt[inc]) if inc is not None else None

    def dma(eng, outap, inap, waits=(), sem=None):
        if sem is None:
            sem = "SQ" if eng == "sync" else "GQ"
        return op(eng, "dma_start", waits=waits, inc=sem, incv=16, out=outap,
                  in_=inap)

    PSUM = es.enter_context(nc.psum_tensor("ps", [128, 8 * 512], F32))
    bank_free = [None] * 8

    def bank(i):
        return PSUM[:, i * 512:(i + 1) * 512]

    def mm(bk_or_ap, lhsT, rhs, start, stop, waits=(), inc_anyway=False):
        o = bank(bk_or_ap) if isinstance(bk_or_ap, int) else bk_or_ap
        return op("tensor", "matmul", o, lhsT, rhs, start=start, stop=stop,
                  waits=waits, inc=("TEN" if (stop or inc_anyway) else None))

    def bc_ap(colap, like):
        b, _ = bass.broadcast_tensor_aps(colap, like)
        return b

    dbc_b = es.enter_context(nc.sbuf_tensor("dbc_b", [128, L], F16))
    gate = es.enter_context(nc.sbuf_tensor("gate", [128, 128], F32))
    A_sb = es.enter_context(nc.sbuf_tensor("A_sb", [128, 256], F32))
    cbb = es.enter_context(nc.sbuf_tensor("cbb", [128, 16], F32))
    dtbb = es.enter_context(nc.sbuf_tensor("dtbb", [128, 16], F32))
    dvb = es.enter_context(nc.sbuf_tensor("dvb", [128, 16], F32))
    onesP = es.enter_context(nc.sbuf_tensor("onesP", [128, 128], F16))
    ones1 = es.enter_context(nc.sbuf_tensor("ones1", [1, 128], F16))
    br0 = es.enter_context(nc.sbuf_tensor("br0", [1, 2, L], F16))
    scol = es.enter_context(nc.sbuf_tensor("scol", [128, 160], F32))
    lgt = es.enter_context(nc.sbuf_tensor("lgt", [128, NTC, 8], F32))
    epsb = es.enter_context(nc.sbuf_tensor("epsb", [128, 1], F32))

    dma("sync", cbb[:], cb)
    dma("sync", dtbb[:], dtb)
    dma("sync", dvb[:], dvec)
    ld_alog = None
    for dc in range(OC):
        ld_alog = dma("sync", A_sb[:, dc * 16:(dc + 1) * 16],
                      alog[dc * 128:(dc + 1) * 128, :])
    ld_small = ld_alog
    a_exp = op("scalar", "activation", A_sb[:], A_sb[:], AF.Exp, bias=0.0,
               scale=1.0, waits=[ld_small], inc="SCA")
    a_neg = op("vector", "tensor_scalar_mul", A_sb[:], A_sb[:], -1.0,
               waits=[a_exp], inc="VEC")
    op("vector", "memset", epsb[:], 1e-5)
    op("vector", "memset", onesP[:], 1.0 / 1024.0)
    ones_done = op("vector", "memset", ones1[:], 1.0, inc="VEC")

    g_xn = contextlib.ExitStack()
    xn_bf = g_xn.enter_context(nc.sbuf_tensor("xn_bf", [128, KC, L], F16))

    g1 = contextlib.ExitStack()
    xstr = g1.enter_context(nc.sbuf_tensor("xstr", [128, 2, KC, 512], F32))
    xbb = g1.enter_context(nc.sbuf_tensor("xbb", [128, KC, 512], F16))
    x2b = g1.enter_context(nc.sbuf_tensor("x2b", [128, KC, 512], F16))
    meanb = g1.enter_context(nc.sbuf_tensor("meanb", [128, 512], F32))
    varb = g1.enter_context(nc.sbuf_tensor("varb", [128, 512], F32))
    stdb = g1.enter_context(nc.sbuf_tensor("stdb", [128, 512], F32))
    rstdb = g1.enter_context(nc.sbuf_tensor("rstdb", [128, 512], F32))
    tmp2 = g1.enter_context(nc.sbuf_tensor("tmp2", [128, 3, 512], F32))
    xnf = g1.enter_context(nc.sbuf_tensor("xnf", [128, KC, 512], F32))
    wrl = g1.enter_context(nc.sbuf_tensor("wrl", [128, KC, 8], F32))

    wr_ld = dma("sync", wrl[:], wrt.rearrange("(j p) e -> p j e", p=128))
    x_fm_r = x_fm.rearrange("(c p) l -> p c l", p=128)
    lg_done = [None] * NTC
    xstr_rd = [None, None]
    xstr_fr = [None, None]
    xstr_fr_s = [None, None]
    mm_mean_done = None
    mm_msq_done = None
    xn_done = None
    for tt in range(NTT):
        s = tt % 2
        xstr_rd[s] = dma("sync", xstr[:, s], x_fm_r[:, :, tt * 512:(tt + 1) * 512],
                         waits=[xstr_fr[s], xstr_fr_s[s],
                                wr_ld if tt == 0 else None],
                         sem=("XQa", "XQb")[s])
        cv = op("vector", "tensor_copy", xbb[:], xstr[:, s],
                waits=[xstr_rd[s], mm_mean_done], inc="VEC")
        sq = op("scalar", "activation", x2b[:], xstr[:, s], AF.Square, bias=0.0,
                scale=1.0, waits=[xstr_rd[s], mm_msq_done], inc="SCA")
        xstr_fr_s[s] = sq
        for kc in range(KC):
            mm_mean_done = mm(0, onesP[:], xbb[:, kc, :], kc == 0, kc == KC - 1,
                              waits=[cv, ones_done, bank_free[0]])
        for kc in range(KC):
            mm_msq_done = mm(1, onesP[:], x2b[:, kc, :], kc == 0, kc == KC - 1,
                             waits=[sq, bank_free[1]])
        mcv = op("vector", "tensor_copy", meanb[:], bank(0), waits=[mm_mean_done],
                 inc="VEC")
        bank_free[0] = mcv
        op("vector", "tensor_copy", varb[:], bank(1), waits=[mm_msq_done])
        va = op("vector", "scalar_tensor_tensor", tmp2[:, 2, :], meanb[:], -1.0,
                meanb[:], ALU.mult, ALU.mult, inc="VEC")
        bank_free[1] = va
        va2 = op("vector", "tensor_add", varb[:], varb[:], tmp2[:, 2, :],
                 inc="VEC")
        sqr = op("scalar", "activation", stdb[:], varb[:], AF.Sqrt,
                 bias=epsb[:, 0:1], scale=1.0, waits=[va2], inc="SCA")
        op("vector", "reciprocal", rstdb[:], stdb[:], waits=[sqr])
        xnf_done = None
        for k in range(KC + 2):
            if k < KC:
                op("vector", "tensor_sub", tmp2[:, k % 2, :], xstr[:, s, k, :],
                   meanb[:])
            if 1 <= k < KC + 1:
                xnf_done = op("vector", "tensor_mul", xnf[:, k - 1, :],
                              tmp2[:, (k - 1) % 2, :], rstdb[:], inc="VEC")
            if k >= 2:
                xn_done = op("vector", "tensor_copy",
                             xn_bf[:, k - 2, tt * 512:(tt + 1) * 512],
                             xnf[:, k - 2, :], inc="VEC")
        xstr_fr[s] = xn_done
        for tcl in range(4):
            tc = tt * 4 + tcl
            bk = 2 + (tcl % 2)
            for kc in range(KC):
                t = mm(bank(bk)[:, 0:8], xnf[:, kc, tcl * 128:(tcl + 1) * 128],
                       wrl[:, kc, :], kc == 0, kc == KC - 1,
                       waits=[xnf_done, wr_ld, bank_free[bk]])
            lg_done[tc] = op("vector", "tensor_copy", lgt[:, tc, :],
                             bank(bk)[:, 0:8], waits=[t], inc="VEC")
            bank_free[bk] = lg_done[tc]
    g1.close()

    g8 = contextlib.ExitStack()
    pbuf = g8.enter_context(nc.sbuf_tensor("pbuf", [128, NTC, 8], F32))
    prbuf = g8.enter_context(nc.sbuf_tensor("prbuf", [128, NTC, 8], F32))
    mk1b = g8.enter_context(nc.sbuf_tensor("mk1b", [128, NTC, 8], F32))
    mk2b = g8.enter_context(nc.sbuf_tensor("mk2b", [128, NTC, 8], F32))
    pmb = g8.enter_context(nc.sbuf_tensor("pmb", [128, NTC, 8], F32))

    def C(i):
        return scol[:, i:i + 1]

    def col(b, tc):
        return C(8 + b * NTC + tc)
    for tc in range(NTC):
        op("vector", "reduce_max", col(0, tc), lgt[:, tc, :], axis=AX.X,
           negate=True, waits=[lg_done[tc]])
    for tc in range(NTC):
        op("vector", "scalar_tensor_tensor", pbuf[:, tc, :], lgt[:, tc, :], 1.0,
           bc_ap(col(0, tc), lgt[:, tc, :]), ALU.mult, ALU.add)
    nm_mark = op("vector", "tensor_copy", C(0), C(0), inc="VEC")
    exp_done = None
    for tc in range(NTC):
        exp_done = op("scalar", "activation", pbuf[:, tc, :], pbuf[:, tc, :],
                      AF.Exp, bias=0.0, scale=1.0, waits=[nm_mark], inc="SCA")
    exp_mark = op("scalar", "copy", C(1), C(1), inc="SCA")
    for tc in range(NTC):
        op("vector", "reduce_sum", col(1, tc), pbuf[:, tc, :], axis=AX.X,
           waits=[exp_mark])
    for tc in range(NTC):
        op("vector", "reciprocal", col(2, tc), col(1, tc))
    for tc in range(NTC):
        op("vector", "tensor_mul", prbuf[:, tc, :], pbuf[:, tc, :],
           bc_ap(col(2, tc), pbuf[:, tc, :]))
    for tc in range(NTC):
        op("vector", "reduce_max", col(3, tc), prbuf[:, tc, :], axis=AX.X)
    for tc in range(NTC):
        op("vector", "tensor_tensor", mk1b[:, tc, :], prbuf[:, tc, :],
           bc_ap(col(3, tc), prbuf[:, tc, :]), ALU.is_ge)
    for tc in range(NTC):
        op("vector", "scalar_tensor_tensor", pmb[:, tc, :], mk1b[:, tc, :],
           -2.0, prbuf[:, tc, :], ALU.mult, ALU.add)
    for tc in range(NTC):
        op("vector", "reduce_max", col(4, tc), pmb[:, tc, :], axis=AX.X)
    for tc in range(NTC):
        op("vector", "tensor_tensor", mk2b[:, tc, :], pmb[:, tc, :],
           bc_ap(col(4, tc), pmb[:, tc, :]), ALU.is_ge)
    for tc in range(NTC):
        op("vector", "tensor_add", col(5, tc), col(3, tc), col(4, tc))
    for tc in range(NTC):
        op("vector", "reciprocal", col(6, tc), col(5, tc))
    for tc in range(NTC):
        op("vector", "tensor_mul", col(7, tc), col(3, tc), col(6, tc))
    for tc in range(NTC):
        op("vector", "tensor_mul", col(8, tc), col(4, tc), col(6, tc))
    for tc in range(NTC):
        op("vector", "tensor_mul", pbuf[:, tc, :], mk1b[:, tc, :],
           bc_ap(col(7, tc), mk1b[:, tc, :]))
    for tc in range(NTC):
        op("vector", "tensor_mul", pmb[:, tc, :], mk2b[:, tc, :],
           bc_ap(col(8, tc), mk2b[:, tc, :]))
    gate_done = None
    for tc in range(NTC):
        gate_done = op("vector", "tensor_add", gate[:, tc * 8:(tc + 1) * 8],
                       pbuf[:, tc, :], pmb[:, tc, :], inc="VEC")
    g8.close()
    if debug:
        dma("gpsimd", dbg["d_gate"], gate[:], waits=[gate_done])
        dma("gpsimd", dbg["d_xn"], xn_bf[:], waits=[xn_done])

    g2 = contextlib.ExitStack()
    dbc_f = g2.enter_context(nc.sbuf_tensor("dbc_f", [128, L], F32))
    xc_bf = g2.enter_context(nc.sbuf_tensor("xc_bf", [128, OC, L], F16))
    g2b = contextlib.ExitStack()
    xi_bf = g2b.enter_context(nc.sbuf_tensor("xi_bf", [128, 8, 3 + L], F16))
    cvtb = g2b.enter_context(nc.sbuf_tensor("cvtb", [128, 4, KC, 512], F16))

    ipw_r = ipw.rearrange("(j p) f -> p j f", p=128)
    CVS = ("CVa", "CVb", "CVc", "CVd")
    cvt_fr = [None] * 4
    last_xc_ev = None

    for g in range(2):
        pad_done = None
        for j in range(8):
            pad_done = op("vector", "memset", xi_bf[:, j, 0:3], 0.0, inc="VEC")
        xi_done = None
        for cb_i in range(2):
            c0 = g * 1024 + cb_i * 512
            s = cb_i % 2
            wcv = dma("gpsimd", cvtb[:, s], ipw_r[:, :, c0:c0 + 512],
                      waits=[cvt_fr[s], gate_done, exp_mark], sem=CVS[s])
            for ocb in range(4):
                oc_local = cb_i * 4 + ocb
                for tt in range(NTT):
                    bk = 4 + ((ocb * NTT + tt) % 4)
                    for j in range(KC):
                        t = mm(bk, cvtb[:, s, j, ocb * 128:(ocb + 1) * 128],
                               xn_bf[:, j, tt * 512:(tt + 1) * 512],
                               j == 0, j == KC - 1,
                               waits=[wcv, xn_done, pad_done, bank_free[bk]])
                    xi_done = op("scalar", "copy",
                                 xi_bf[:, oc_local, 3 + tt * 512:3 + (tt + 1) * 512],
                                 bank(bk), waits=[t], inc="SCA")
                    bank_free[bk] = xi_done
            cvt_fr[s] = t
        for ob in range(2):
            o0 = g * 1024 + ob * 512
            wcvs = []
            for w in range(4):
                wcv = dma("gpsimd", cvtb[:, w],
                          cw[w].rearrange("(j p) f -> p j f", p=128)[:, :, o0:o0 + 512],
                          waits=[cvt_fr[w]], sem=CVS[w])
                wcvs.append(wcv)
            for ocb in range(4):
                oc_g = g * 8 + ob * 4 + ocb
                for tt in range(NTT):
                    bk = 4 + ((ocb * NTT + tt) % 4)
                    first = True
                    for w in range(4):
                        for j in range(8):
                            t = mm(bk, cvtb[:, w, j, ocb * 128:(ocb + 1) * 128],
                                   xi_bf[:, j, w + tt * 512:w + (tt + 1) * 512],
                                   first, (w == 3 and j == 7),
                                   waits=[wcvs[w], xi_done, bank_free[bk]])
                            first = False
                    last_xc_ev = op("scalar", "activation",
                                    xc_bf[:, oc_g, tt * 512:(tt + 1) * 512],
                                    bank(bk), AF.Silu,
                                    bias=cbb[:, oc_g:oc_g + 1], scale=1.0,
                                    waits=[t, ld_small], inc="SCA")
                    bank_free[bk] = last_xc_ev
            for w in range(4):
                cvt_fr[w] = t
    xc_done = last_xc_ev
    p23_ten = ("TEN", cnt["TEN"])
    p23_vec = ("VEC", cnt["VEC"])
    g2b.close()

    xpl = g2.enter_context(nc.sbuf_tensor("xpl", [128, OC, 96], F32))
    xpb = g2.enter_context(nc.sbuf_tensor("xpb", [128, OC, 96], F16))
    r = dma("sync", xpl[:], xpw.rearrange("(j p) f -> p j f", p=128),
            waits=[p23_ten, p23_vec])
    xp_cv = op("vector", "tensor_copy", xpb[:], xpl[:], waits=[r], inc="VEC")
    for tt in range(NTT):
        bk = tt % 2
        for j in range(OC):
            t = mm(bank(bk)[0:96, :], xpb[:, j, :],
                   xc_bf[:, j, tt * 512:(tt + 1) * 512], j == 0, j == OC - 1,
                   waits=[xp_cv, xc_done, bank_free[bk]])
        ev = op("vector", "tensor_copy", dbc_f[0:96, tt * 512:(tt + 1) * 512],
                bank(bk)[0:96, :], waits=[t], inc="VEC")
        bank_free[bk] = ev
    dbc_cv = op("vector", "tensor_copy", dbc_b[0:96, :], dbc_f[0:96, :],
                inc="VEC")
    dbc_guard = op("vector", "tensor_copy", dbc_b[0:64, 0:512],
                   dbc_b[0:64, 0:512], inc="VEC")
    bc_rows = dma("gpsimd", sc_bc, dbc_b[64:96, :], waits=[dbc_guard])
    if debug:
        dma("gpsimd", dbg["d_dbc"], dbc_f[:], waits=[dbc_cv])

    dtl = g2.enter_context(nc.sbuf_tensor("dtl", [64, DI], F32))
    dtwb = g2.enter_context(nc.sbuf_tensor("dtwb", [64, DI], F16))
    sp1 = g2.enter_context(nc.sbuf_tensor("sp1", [128, 512], F32))
    sp2 = g2.enter_context(nc.sbuf_tensor("sp2", [128, 512], F32))
    dstage = g2.enter_context(nc.sbuf_tensor("dstage", [128, 3, 512], F16))
    r = dma("sync", dtl[:], dtw, waits=[xp_cv])
    dt_cv = op("vector", "tensor_copy", dtwb[:], dtl[:], waits=[r], inc="VEC")
    d_spill = [[None] * NTT for _ in range(OC)]
    xc_spill = [None] * OC
    dst_fr = [None, None, None]
    sp_fr = None
    pend = None
    for ocg in range(OC):
        for tt in range(NTT):
            bk = 2 + ((ocg * NTT + tt) % 2)
            t = mm(bk, dtwb[:, ocg * 128:(ocg + 1) * 128],
                   dbc_b[0:64, tt * 512:(tt + 1) * 512], True, True,
                   waits=[dt_cv, dbc_cv, bank_free[bk]])
            bc = dtbb[:, ocg:ocg + 1]
            op("scalar", "activation", sp1[:], bank(bk), AF.Abs, bias=bc,
               scale=1.0, waits=[t, sp_fr])
            op("scalar", "activation", sp1[:], sp1[:], AF.Exp, bias=0.0,
               scale=-1.0)
            op("scalar", "activation", sp1[:], sp1[:], AF.Ln, bias=1.0, scale=1.0)
            ab = op("scalar", "activation", sp2[:], bank(bk), AF.Relu, bias=bc,
                    scale=1.0, inc="SCA")
            bank_free[bk] = ab
            sl = (ocg * NTT + tt) % 3
            dv = op("vector", "tensor_add", dstage[:, sl, :], sp1[:], sp2[:],
                    waits=[ab, dst_fr[sl]], inc="VEC")
            sp_fr = dv
            if pend is not None:
                po, pt, psl = pend
                d_spill[po][pt] = dma("gpsimd",
                                      sc_delta[po][:, pt * 512:(pt + 1) * 512],
                                      dstage[:, psl, :], waits=[dv])
                dst_fr[psl] = d_spill[po][pt]
            pend = (ocg, tt, sl)
        xc_spill[ocg] = dma("gpsimd", sc_xc[ocg], xc_bf[:, ocg, :],
                            waits=[xc_done, ("SCA", cnt["SCA"])])
    guard = op("vector", "tensor_copy", sp1[:], sp1[:], inc="VEC")
    po, pt, psl = pend
    d_spill[po][pt] = dma("gpsimd", sc_delta[po][:, pt * 512:(pt + 1) * 512],
                          dstage[:, psl, :], waits=[guard])
    xn_spill = dma("gpsimd", sc_xn, xn_bf[:], waits=[xn_done, guard])
    gq_spills = ("GQ", cnt["GQ"])
    p45_ten = ("TEN", cnt["TEN"])
    p45_vec = ("VEC", cnt["VEC"])
    p45_sca = ("SCA", cnt["SCA"])
    g2.close()
    g_xn.close()
    if debug:
        dma("gpsimd", dbg["d_delta"], sc_delta, waits=[gq_spills])
        dma("gpsimd", dbg["d_xc"], sc_xc, waits=[gq_spills])

    g_yac = contextlib.ExitStack()
    yac = g_yac.enter_context(nc.sbuf_tensor("yac", [128, OC, L], F16))
    g3 = contextlib.ExitStack()
    bbn = g3.enter_context(nc.sbuf_tensor("bbn", [128, 2, L], F16))
    ccn = g3.enter_context(nc.sbuf_tensor("ccn", [128, 2, L], F16))
    dAb = g3.enter_context(nc.sbuf_tensor("dAb", [128, 3, L], F16))
    dBub = g3.enter_context(nc.sbuf_tensor("dBub", [128, 2, L], F16))
    hbf = g3.enter_context(nc.sbuf_tensor("hbf", [128, L], F16))
    ytm = g3.enter_context(nc.sbuf_tensor("ytm", [128, L], F16))

    bc_ev = [None, None]
    bc_used = [None, None]
    half_sca = None

    def build_bc(n, slot, extra=()):
        b1, _ = bass.broadcast_tensor_aps(sc_bc[n:n + 1, :], bbn[:, slot, :])
        c1, _ = bass.broadcast_tensor_aps(sc_bc[16 + n:17 + n, :], ccn[:, slot, :])
        dma("gpsimd", bbn[:, slot, :], b1,
            waits=[bc_rows, bc_used[slot]] + list(extra))
        bc_ev[slot] = dma("gpsimd", ccn[:, slot, :], c1, waits=[])

    half_ev = None
    for dh in range(NDH):
        g3h = contextlib.ExitStack()
        dlt = g3h.enter_context(nc.sbuf_tensor(f"dlt{dh}", [128, DCH, L], F16))
        xcs = g3h.enter_context(nc.sbuf_tensor(f"xcs{dh}", [128, DCH, L], F16))
        dub = xcs
        for i in range(DCH):
            dc = dh * DCH + i
            dma("sync", dlt[:, i, :], sc_delta[dc],
                waits=[gq_spills, half_ev, half_sca, p45_vec, p45_sca, p45_ten])
            dma("sync", xcs[:, i, :], sc_xc[dc], waits=[])
        all_loads = ("SQ", cnt["SQ"])
        du_ev = [None] * DCH
        drd = [all_loads] * DCH
        for i in range(DCH):
            dc = dh * DCH + i
            op("vector", "tensor_scalar", yac[:, dc, :], xcs[:, i, :],
               dvb[:, dc:dc + 1], None, ALU.mult, waits=[all_loads])
            du_ev[i] = op("vector", "tensor_mul", xcs[:, i, :], dlt[:, i, :],
                          xcs[:, i, :], inc="VEC")
        if dh == 0:
            build_bc(0, 0, extra=[dbc_cv, ("VEC", cnt["VEC"])])
        else:
            build_bc(0, 0)

        NSL = NS * DCH
        da_ev = [None] * (NSL + 4)
        scan_ev = [None] * NSL

        def emit_da(k):
            if k >= NSL + 2:
                return
            if k >= NSL:
                da_ev[k] = op("scalar", "copy", dAb[:, k % 3, :],
                              dAb[:, k % 3, :],
                              waits=[scan_ev[k - 3] if k >= 3 else None],
                              inc="SCA")
                return
            n, i_loc = divmod(k, DCH)
            dc = dh * DCH + i_loc
            if vecda:
                da_ev[k] = op("vector", "tensor_scalar_mul", dAb[:, k % 3, :],
                              dlt[:, i_loc, :], 0.1,
                              waits=[drd[i_loc], a_neg,
                                     scan_ev[k - 3] if k >= 3 else None],
                              inc="SCA")
            elif novecda:
                da_ev[k] = op("scalar", "mul", dAb[:, k % 3, :],
                              dlt[:, i_loc, :], 0.1,
                              waits=[drd[i_loc], a_neg,
                                     scan_ev[k - 3] if k >= 3 else None],
                              inc="SCA")
            else:
                da_ev[k] = op("scalar", "activation", dAb[:, k % 3, :],
                              dlt[:, i_loc, :], AF.Exp, bias=0.0,
                              scale=A_sb[:, dc * 16 + n:dc * 16 + n + 1],
                              waits=[drd[i_loc], a_neg,
                                     scan_ev[k - 3] if k >= 3 else None],
                              inc="SCA")

        emit_da(0)
        emit_da(1)
        op("vector", "tensor_mul", dBub[:, 0, :], dub[:, 0, :], bbn[:, 0, :],
           waits=[bc_ev[0], du_ev[DCH - 1]])
        for k in range(NSL):
            n, i_loc = divmod(k, DCH)
            dc = dh * DCH + i_loc
            ns = n % 2
            emit_da(k + 2)
            if i_loc == 1 and n + 1 < NS:
                build_bc(n + 1, (n + 1) % 2)
            if noscan:
                scan_ev[k] = op("vector", "tensor_mul", hbf[:],
                                dAb[:, k % 3, :], dBub[:, k % 2, :],
                                waits=[da_ev[k + 2]], inc="VEC")
            else:
                scan_ev[k] = op("vector", "tensor_tensor_scan", hbf[:],
                                dAb[:, k % 3, :], dBub[:, k % 2, :], 0.0,
                                ALU.mult, ALU.add, waits=[da_ev[k + 2]],
                                inc="VEC")
            if k + 1 < NSL:
                n2, i2 = divmod(k + 1, DCH)
                op("vector", "tensor_mul", dBub[:, (k + 1) % 2, :],
                   dub[:, i2, :], bbn[:, n2 % 2, :],
                   waits=[bc_ev[n2 % 2]])
            if ytmp_sep:
                op("vector", "tensor_mul", ytm[:], hbf[:], ccn[:, ns, :])
                op("vector", "tensor_add", yac[:, dc, :], yac[:, dc, :], ytm[:],
                   inc="VEC")
            else:
                op("vector", "tensor_mul", hbf[:], hbf[:], ccn[:, ns, :])
                op("vector", "tensor_add", yac[:, dc, :], yac[:, dc, :], hbf[:],
                   inc="VEC")
            if i_loc == 0:
                bc_used[(n + 1) % 2] = ("VEC", cnt["VEC"])
        half_ev = ("VEC", cnt["VEC"])
        half_sca = ("SCA", cnt["SCA"])
        bc_used[0] = bc_used[1] = half_ev
        g3h.close()
    scan_vec = ("VEC", cnt["VEC"])
    scan_sca = ("SCA", cnt["SCA"])
    scan_ten = ("TEN", cnt["TEN"])
    g3.close()

    g5 = contextlib.ExitStack()
    xn2 = g5.enter_context(nc.sbuf_tensor("xn2", [128, KC, L], F16))
    opw_b = g5.enter_context(nc.sbuf_tensor("opw_b", [128, OC, D], F16))
    rland = g5.enter_context(nc.sbuf_tensor("rland", [128, 2, KC, 128], F32))
    rbf = g5.enter_context(nc.sbuf_tensor("rbf", [128, 2, KC, 128], F16))
    ressl = g5.enter_context(nc.sbuf_tensor("ressl", [128, 2, L], F16))
    xch = g5.enter_context(nc.sbuf_tensor("xch", [128, 2, D], F32))

    xn_rd = dma("gpsimd", xn2[:], sc_xn,
                waits=[xn_spill, scan_vec, scan_sca, scan_ten])
    with nc.sbuf_tensor("oland", [128, 4, D], F32) as oland:
        opw_r = opw.rearrange("(c p) d -> p c d", p=128)
        opcv = None
        for blk in range(4):
            r = dma("sync", oland[:], opw_r[:, blk * 4:(blk + 1) * 4, :],
                    waits=[opcv, scan_vec])
            opcv = op("vector", "tensor_copy", opw_b[:, blk * 4:(blk + 1) * 4, :],
                      oland[:], waits=[r], inc="VEC")
        opw_done = opcv

    res_fr = [None, None]
    res_used = [None, None]
    rsl_ev = [None] * (OC + 1)
    ys_ev = [None] * OC

    def emit_res(dc):
        if dc >= OC:
            if dc == OC:
                rsl_ev[OC] = op("scalar", "copy", ressl[:, (OC - 1) % 2, 0:512],
                                ressl[:, (OC - 1) % 2, 0:512], inc="SCA")
            return
        rs = dc % 2
        rr = dma("sync", rland[:, rs],
                 ipw_r[:, :, DI + dc * 128:DI + (dc + 1) * 128],
                 waits=[res_fr[rs]], sem=("RQa", "RQb")[rs])
        rcv = op("vector", "tensor_copy", rbf[:, rs], rland[:, rs],
                 waits=[rr, res_used[rs]], inc="VEC")
        res_fr[rs] = rcv
        bk = 2 + rs
        for q in range(NTT):
            for j in range(KC):
                t = mm(bk, rbf[:, rs, j, :],
                       xn2[:, j, q * 512:(q + 1) * 512], j == 0, j == KC - 1,
                       waits=[rcv, xn_rd, bank_free[bk]])
            rsl_ev[dc] = op("scalar", "activation",
                            ressl[:, rs, q * 512:(q + 1) * 512], bank(bk),
                            AF.Silu, bias=0.0, scale=1.0, waits=[t], inc="SCA")
            bank_free[bk] = rsl_ev[dc]
        res_used[rs] = t

    emit_res(0)
    for dc in range(OC):
        emit_res(dc + 1)
        ys_ev[dc] = op("vector", "tensor_mul", yac[:, dc, :], yac[:, dc, :],
                       ressl[:, dc % 2, :], waits=[rsl_ev[dc + 1]], inc="VEC")
    ys_done = ys_ev[OC - 1]

    mamba_flush = [None] * NTC
    pend_fl = None
    add_ev = None
    for tc in range(NTC):
        xs = tc % 2
        prev_gq = ("GQ", cnt["GQ"]) if tc >= 2 else None
        xr = dma("sync", xch[:, xs, :], x_tm[tc * 128:(tc + 1) * 128, :],
                 waits=[prev_gq], sem=("XCa", "XCb")[xs])
        for dh2 in range(2):
            bk = 4 + ((tc * 2 + dh2) % 4)
            for ocg in range(OC):
                t = mm(bk, yac[:, ocg, tc * 128:(tc + 1) * 128],
                       opw_b[:, ocg, dh2 * 512:(dh2 + 1) * 512],
                       ocg == 0, ocg == OC - 1,
                       waits=[ys_done, opw_done, bank_free[bk]])
            add_ev = op("vector", "tensor_add",
                        xch[:, xs, dh2 * 512:(dh2 + 1) * 512],
                        xch[:, xs, dh2 * 512:(dh2 + 1) * 512], bank(bk),
                        waits=[t, xr], inc="VEC")
            bank_free[bk] = add_ev
            if pend_fl is not None and dh2 == 0:
                ptc, pxs = pend_fl
                mamba_flush[ptc] = dma("gpsimd", sc_mamba[ptc], xch[:, pxs, :],
                                       waits=[add_ev])
                pend_fl = None
        pend_fl = (tc, xs)
    fin_guard = op("vector", "tensor_copy", xch[:, (NTC - 1) % 2, 0:512],
                   xch[:, (NTC - 1) % 2, 0:512], inc="VEC")
    ptc, pxs = pend_fl
    mamba_flush[ptc] = dma("gpsimd", sc_mamba[ptc], xch[:, pxs, :],
                           waits=[fin_guard])
    p7_ten = ("TEN", cnt["TEN"])
    p7_vec = ("VEC", cnt["VEC"])
    p7_sca = ("SCA", cnt["SCA"])
    if debug:
        dma("gpsimd", dbg["d_mamba"], sc_mamba, waits=[("GQ", cnt["GQ"])])
        dma("gpsimd", dbg["d_ys"], yac[:], waits=[fin_guard])
    g5.close()
    g_yac.close()

    g4 = contextlib.ExitStack()
    xn3 = g4.enter_context(nc.sbuf_tensor("xn3", [128, KC, L], F16))
    h_bf = g4.enter_context(nc.sbuf_tensor("h_bf", [128, FCN, 1024], F16))
    w1b = g4.enter_context(nc.sbuf_tensor("w1b", [128, 2, KC, 512], F16))
    w2dh = g4.enter_context(nc.sbuf_tensor("w2dh", [128, 2, FCN // 2, 512], F16))
    out_h = g4.enter_context(nc.sbuf_tensor("out_h", [128, 8, D], F32))

    xn_rd2 = dma("gpsimd", xn3[:], sc_xn,
                 waits=[p7_ten, p7_vec, p7_sca, ("GQ", cnt["GQ"])])
    w1b_fr = [None, None]
    w2_fr = [None, None]
    w2_rd = [None, None]
    h_free = None
    final_flush = None
    ev = None
    w2_r = w2t.rearrange("e (f p) d -> e p f d", p=128)

    for th in range(2):
        init_r = []
        for tci in range(8):
            tc = th * 8 + tci
            init_r.append(dma("gpsimd", out_h[:, tci, :], sc_mamba[tc],
                              waits=[("GQ", cnt["GQ"]), p7_vec, p7_sca]))
        for e in range(NE):
            w1e = w1t[e].rearrange("(j p) f -> p j f", p=128)
            h_done = None

            def w2q(qi):
                dh2, fh = divmod(qi, 2)
                sl = qi % 2
                w2_rd[sl] = dma("gpsimd", w2dh[:, sl],
                                w2_r[e][:, fh * 16:(fh + 1) * 16,
                                        dh2 * 512:(dh2 + 1) * 512],
                                waits=[w2_fr[sl]], sem=("W2a", "W2b")[sl])
            w2q(0)
            w2q(1)
            for fb in range(8):
                s = fb % 2
                wcv = dma("gpsimd", w1b[:, s],
                          w1e[:, :, fb * 512:(fb + 1) * 512],
                          waits=[w1b_fr[s]], sem=("WQa", "WQb")[s])
                for fcb in range(4):
                    fc = fb * 4 + fcb
                    for tt in range(2):
                        bk = (fcb * 2 + tt) % 4
                        for j in range(KC):
                            t = mm(bk, w1b[:, s, j, fcb * 128:(fcb + 1) * 128],
                                   xn3[:, j, th * 1024 + tt * 512:
                                       th * 1024 + (tt + 1) * 512],
                                   j == 0, j == KC - 1,
                                   waits=[wcv, xn_rd2, h_free, bank_free[bk]])
                        if fcb == 3 and tt == 1:
                            w1b_fr[s] = t
                        h_done = op("scalar", "activation",
                                    h_bf[:, fc, tt * 512:(tt + 1) * 512],
                                    bank(bk), AF.Silu, bias=0.0, scale=1.0,
                                    waits=[t], inc="SCA")
                        bank_free[bk] = h_done

            for dh2 in range(2):
                t_grp = None
                for fc in range(FCN):
                    fh, fl = divmod(fc, 16)
                    qi = dh2 * 2 + fh
                    sl = qi % 2
                    if fl == 0 and 1 <= qi < 3:
                        w2q(qi + 1)
                    for tci in range(8):
                        t_grp = mm(tci, h_bf[:, fc, tci * 128:(tci + 1) * 128],
                                   w2dh[:, sl, fl, :], fc == 0, fc == FCN - 1,
                                   waits=[w2_rd[sl], h_done, bank_free[tci]],
                                   inc_anyway=(tci == 7 and fc % 8 == 7))
                    if fl == 15:
                        w2_fr[sl] = t_grp
                for tci in range(8):
                    tc = th * 8 + tci
                    gcol = gate[:, tc * 8 + e:tc * 8 + e + 1]
                    ev = op("vector", "scalar_tensor_tensor",
                            out_h[:, tci, dh2 * 512:(dh2 + 1) * 512],
                            bank(tci), gcol,
                            out_h[:, tci, dh2 * 512:(dh2 + 1) * 512],
                            ALU.mult, ALU.add,
                            waits=[t_grp, gate_done, init_r[tci]], inc="VEC")
                    bank_free[tci] = ev
                h_free = t_grp

        flg = op("vector", "tensor_copy", h_bf[:, 0, :], h_bf[:, 0, :],
                 waits=[ev], inc="VEC")
        for tci in range(8):
            tc = th * 8 + tci
            final_flush = dma("gpsimd", out[tc * 128:(tc + 1) * 128, :],
                              out_h[:, tci, :], waits=[flg])

    g4.close()

    def fin(e_, s_=sems["GQ"], v=final_flush[1]):
        e_.wait_ge(s_, v)
    ops["gpsimd"].append(fin)

    with nc.Block() as block:
        @block.tensor
        def _(e):
            for f in ops["tensor"]:
                f(e)

        @block.vector
        def _(e):
            for f in ops["vector"]:
                f(e)

        @block.scalar
        def _(e):
            for f in ops["scalar"]:
                f(e)

        @block.gpsimd
        def _(e):
            for f in ops["gpsimd"]:
                f(e)

        @block.sync
        def _(e):
            for f in ops["sync"]:
                f(e)

    es.close()
    return nc


_NC_CACHE = {}


def _get_nc(debug=False):
    if debug not in _NC_CACHE:
        _NC_CACHE[debug] = build(debug)
    return _NC_CACHE[debug]


def make_in_maps(x, ipw, cw, cbv, xpw, dtwv, dtbv, alogv, dvecv, opwv, W1, W2, Wr):
    shared = {
        "ipw": np.ascontiguousarray(ipw, dtype=np.float32),
        "cw": np.ascontiguousarray(cw, dtype=np.float32),
        "cb": np.ascontiguousarray(np.asarray(cbv, np.float32).reshape(16, 128).T),
        "xpw": np.ascontiguousarray(xpw, dtype=np.float32),
        "dtw": np.ascontiguousarray(dtwv, dtype=np.float32),
        "dtb": np.ascontiguousarray(np.asarray(dtbv, np.float32).reshape(16, 128).T),
        "alog": np.ascontiguousarray(alogv, dtype=np.float32),
        "dvec": np.ascontiguousarray(np.asarray(dvecv, np.float32).reshape(16, 128).T),
        "opw": np.ascontiguousarray(opwv, dtype=np.float32),
        "w1t": np.ascontiguousarray(np.asarray(W1, np.float32).transpose(0, 2, 1)),
        "w2t": np.ascontiguousarray(np.asarray(W2, np.float32).transpose(0, 2, 1)),
        "wrt": np.ascontiguousarray(np.asarray(Wr, np.float32).T),
    }
    in_maps = []
    for c in range(x.shape[0]):
        m = dict(shared)
        m["x_tm"] = np.ascontiguousarray(x[c])
        m["x_fm"] = np.ascontiguousarray(x[c].T)
        in_maps.append(m)
    return in_maps


def kernel(L0_attn_x, L0_attn_in_proj_w, L0_attn_conv_w, L0_attn_conv_b,
           L0_attn_x_proj_w, L0_attn_dt_proj_w, L0_attn_dt_proj_b,
           L0_attn_A_log, L0_attn_D, L0_attn_out_proj_w,
           L0_ffn_W1, L0_ffn_W2, L0_ffn_Wr):
    x = np.ascontiguousarray(np.asarray(L0_attn_x, dtype=np.float32))
    in_maps = make_in_maps(x, L0_attn_in_proj_w, L0_attn_conv_w, L0_attn_conv_b,
                           L0_attn_x_proj_w, L0_attn_dt_proj_w, L0_attn_dt_proj_b,
                           L0_attn_A_log, L0_attn_D, L0_attn_out_proj_w,
                           L0_ffn_W1, L0_ffn_W2, L0_ffn_Wr)
    nc = _get_nc(False)
    res = run_bass_kernel_spmd(nc, in_maps, core_ids=list(range(8)))
    return np.stack([res.results[c]["out"] for c in range(8)], axis=0)


# version 05
# speedup vs baseline: 1.1652x; 1.0211x over previous
import contextlib
import numpy as np

import concourse.bass as bass
import concourse.mybir as mybir
from concourse.bass_utils import run_bass_kernel_spmd

F32 = mybir.dt.float32
F16 = mybir.dt.float16
AF = mybir.ActivationFunctionType
ALU = mybir.AluOpType
AX = mybir.AxisListType

D = 1024
L = 2048
DI = 2048
NS = 16
DTR = 64
NE = 8
DFF = 4096
KC = D // 128
OC = DI // 128
FCN = DFF // 128
NTT = L // 512
NTC = L // 128
NDH = 2
DCH = OC // NDH


def build(debug=False, noscan=False, novecda=False, vecda=False, nobc=False, ytmp_sep=True):
    nc = bass.Bass("TRN2", target_bir_lowering=False)

    x_tm = nc.dram_tensor("x_tm", [L, D], F32, kind="ExternalInput").ap()
    x_fm = nc.dram_tensor("x_fm", [D, L], F32, kind="ExternalInput").ap()
    ipw = nc.dram_tensor("ipw", [D, 2 * DI], F32, kind="ExternalInput").ap()
    cw = nc.dram_tensor("cw", [4, D, DI], F32, kind="ExternalInput").ap()
    cb = nc.dram_tensor("cb", [128, 16], F32, kind="ExternalInput").ap()
    xpw = nc.dram_tensor("xpw", [DI, 96], F32, kind="ExternalInput").ap()
    dtw = nc.dram_tensor("dtw", [DTR, DI], F32, kind="ExternalInput").ap()
    dtb = nc.dram_tensor("dtb", [128, 16], F32, kind="ExternalInput").ap()
    alog = nc.dram_tensor("alog", [DI, NS], F32, kind="ExternalInput").ap()
    dvec = nc.dram_tensor("dvec", [128, 16], F32, kind="ExternalInput").ap()
    opw = nc.dram_tensor("opw", [DI, D], F32, kind="ExternalInput").ap()
    w1t = nc.dram_tensor("w1t", [NE, D, DFF], F32, kind="ExternalInput").ap()
    w2t = nc.dram_tensor("w2t", [NE, DFF, D], F32, kind="ExternalInput").ap()
    wrt = nc.dram_tensor("wrt", [D, NE], F32, kind="ExternalInput").ap()
    out = nc.dram_tensor("out", [L, D], F32, kind="ExternalOutput").ap()

    sc_delta = nc.dram_tensor("sc_delta", [OC, 128, L], F16, kind="Internal").ap()
    sc_xc = nc.dram_tensor("sc_xc", [OC, 128, L], F16, kind="Internal").ap()
    sc_xn = nc.dram_tensor("sc_xn", [128, KC, L], F16, kind="Internal").ap()
    sc_mamba = nc.dram_tensor("sc_mamba", [NTC, 128, D], F32, kind="Internal").ap()
    sc_bc = nc.dram_tensor("sc_bc", [32, L], F16, kind="Internal").ap()

    dbg = {}
    if debug:
        for name, shape, dt in [
            ("d_xn", [128, KC, L], F16), ("d_xc", [OC, 128, L], F16),
            ("d_dbc", [128, L], F32), ("d_gate", [128, 128], F32),
            ("d_mamba", [NTC, 128, D], F32), ("d_delta", [OC, 128, L], F16),
            ("d_ys", [128, OC, L], F16),
        ]:
            dbg[name] = nc.dram_tensor(name, shape, dt, kind="ExternalOutput").ap()

    ops = {k: [] for k in ("tensor", "vector", "scalar", "gpsimd", "sync")}
    last_waited = {k: {} for k in ops}
    es = contextlib.ExitStack()
    sems = {}
    cnt = {}
    for sname in ("SQ", "GQ", "TEN", "VEC", "SCA", "GPC", "WQa", "WQb",
                  "W2a", "W2b", "RQa", "RQb", "XQa", "XQb", "XCa", "XCb",
                  "CVa", "CVb", "CVc", "CVd"):
        sems[sname] = es.enter_context(nc.semaphore(sname))
        cnt[sname] = 0

    def op(eng, meth, *a, waits=(), inc=None, incv=1, **kw):
        wl = []
        for wv in waits:
            if wv is None:
                continue
            sn, val = wv
            if val is None:
                continue
            if last_waited[eng].get(sn, -1) >= val:
                continue
            last_waited[eng][sn] = val
            wl.append((sems[sn], val))
        if inc is not None:
            cnt[inc] += incv
        semh = sems[inc] if inc is not None else None

        def f(e, meth=meth, a=a, kw=kw, wl=wl, semh=semh, incv=incv):
            for (s, v) in wl:
                e.wait_ge(s, v)
            r = getattr(e, meth)(*a, **kw)
            if semh is not None:
                r.then_inc(semh, incv)
        ops[eng].append(f)
        return (inc, cnt[inc]) if inc is not None else None

    def dma(eng, outap, inap, waits=(), sem=None):
        if sem is None:
            sem = "SQ" if eng == "sync" else "GQ"
        return op(eng, "dma_start", waits=waits, inc=sem, incv=16, out=outap,
                  in_=inap)

    PSUM = es.enter_context(nc.psum_tensor("ps", [128, 8 * 512], F32))
    bank_free = [None] * 8

    def bank(i):
        return PSUM[:, i * 512:(i + 1) * 512]

    def mm(bk_or_ap, lhsT, rhs, start, stop, waits=(), inc_anyway=False):
        o = bank(bk_or_ap) if isinstance(bk_or_ap, int) else bk_or_ap
        return op("tensor", "matmul", o, lhsT, rhs, start=start, stop=stop,
                  waits=waits, inc=("TEN" if (stop or inc_anyway) else None))

    def bc_ap(colap, like):
        b, _ = bass.broadcast_tensor_aps(colap, like)
        return b

    dbc_b = es.enter_context(nc.sbuf_tensor("dbc_b", [128, L], F16))
    gate = es.enter_context(nc.sbuf_tensor("gate", [128, 128], F32))
    A_sb = es.enter_context(nc.sbuf_tensor("A_sb", [128, 256], F32))
    cbb = es.enter_context(nc.sbuf_tensor("cbb", [128, 16], F32))
    dtbb = es.enter_context(nc.sbuf_tensor("dtbb", [128, 16], F32))
    dvb = es.enter_context(nc.sbuf_tensor("dvb", [128, 16], F32))
    onesP = es.enter_context(nc.sbuf_tensor("onesP", [128, 128], F16))
    ones1 = es.enter_context(nc.sbuf_tensor("ones1", [1, 128], F16))
    br0 = es.enter_context(nc.sbuf_tensor("br0", [1, 2, L], F16))
    scol = es.enter_context(nc.sbuf_tensor("scol", [128, 160], F32))
    lgt = es.enter_context(nc.sbuf_tensor("lgt", [128, NTC, 8], F32))
    epsb = es.enter_context(nc.sbuf_tensor("epsb", [128, 1], F32))

    dma("sync", cbb[:], cb)
    dma("sync", dtbb[:], dtb)
    dma("sync", dvb[:], dvec)
    ld_alog = None
    for dc in range(OC):
        ld_alog = dma("sync", A_sb[:, dc * 16:(dc + 1) * 16],
                      alog[dc * 128:(dc + 1) * 128, :])
    ld_small = ld_alog
    a_exp = op("scalar", "activation", A_sb[:], A_sb[:], AF.Exp, bias=0.0,
               scale=1.0, waits=[ld_small], inc="SCA")
    a_neg = op("vector", "tensor_scalar_mul", A_sb[:], A_sb[:], -1.0,
               waits=[a_exp], inc="VEC")
    op("vector", "memset", epsb[:], 1e-5)
    op("vector", "memset", onesP[:], 1.0 / 1024.0)
    ones_done = op("vector", "memset", ones1[:], 1.0, inc="VEC")

    g_xn = contextlib.ExitStack()
    xn_bf = g_xn.enter_context(nc.sbuf_tensor("xn_bf", [128, KC, L], F16))

    g1 = contextlib.ExitStack()
    xstr = g1.enter_context(nc.sbuf_tensor("xstr", [128, 2, KC, 512], F32))
    xbb = g1.enter_context(nc.sbuf_tensor("xbb", [128, KC, 512], F16))
    x2b = g1.enter_context(nc.sbuf_tensor("x2b", [128, KC, 512], F16))
    meanb = g1.enter_context(nc.sbuf_tensor("meanb", [128, 512], F32))
    varb = g1.enter_context(nc.sbuf_tensor("varb", [128, 512], F32))
    stdb = g1.enter_context(nc.sbuf_tensor("stdb", [128, 512], F32))
    rstdb = g1.enter_context(nc.sbuf_tensor("rstdb", [128, 512], F32))
    tmp2 = g1.enter_context(nc.sbuf_tensor("tmp2", [128, 3, 512], F32))
    xnf = g1.enter_context(nc.sbuf_tensor("xnf", [128, KC, 512], F32))
    wrl = g1.enter_context(nc.sbuf_tensor("wrl", [128, KC, 8], F32))

    wr_ld = dma("sync", wrl[:], wrt.rearrange("(j p) e -> p j e", p=128))
    x_fm_r = x_fm.rearrange("(c p) l -> p c l", p=128)
    lg_done = [None] * NTC
    xstr_rd = [None, None]
    xstr_fr = [None, None]
    xstr_fr_s = [None, None]
    mm_mean_done = None
    mm_msq_done = None
    xn_done = None
    for tt in range(NTT):
        s = tt % 2
        xstr_rd[s] = dma("sync", xstr[:, s], x_fm_r[:, :, tt * 512:(tt + 1) * 512],
                         waits=[xstr_fr[s], xstr_fr_s[s],
                                wr_ld if tt == 0 else None],
                         sem=("XQa", "XQb")[s])
        cv = op("vector", "tensor_copy", xbb[:], xstr[:, s],
                waits=[xstr_rd[s], mm_mean_done], inc="VEC")
        sq = op("scalar", "activation", x2b[:], xstr[:, s], AF.Square, bias=0.0,
                scale=1.0, waits=[xstr_rd[s], mm_msq_done], inc="SCA")
        xstr_fr_s[s] = sq
        for kc in range(KC):
            mm_mean_done = mm(0, onesP[:], xbb[:, kc, :], kc == 0, kc == KC - 1,
                              waits=[cv, ones_done, bank_free[0]])
        for kc in range(KC):
            mm_msq_done = mm(1, onesP[:], x2b[:, kc, :], kc == 0, kc == KC - 1,
                             waits=[sq, bank_free[1]])
        mcv = op("vector", "tensor_copy", meanb[:], bank(0), waits=[mm_mean_done],
                 inc="VEC")
        bank_free[0] = mcv
        op("vector", "tensor_copy", varb[:], bank(1), waits=[mm_msq_done])
        va = op("vector", "scalar_tensor_tensor", tmp2[:, 2, :], meanb[:], -1.0,
                meanb[:], ALU.mult, ALU.mult, inc="VEC")
        bank_free[1] = va
        va2 = op("vector", "tensor_add", varb[:], varb[:], tmp2[:, 2, :],
                 inc="VEC")
        sqr = op("scalar", "activation", stdb[:], varb[:], AF.Sqrt,
                 bias=epsb[:, 0:1], scale=1.0, waits=[va2], inc="SCA")
        op("vector", "reciprocal", rstdb[:], stdb[:], waits=[sqr])
        xnf_done = None
        for k in range(KC + 2):
            if k < KC:
                op("vector", "tensor_sub", tmp2[:, k % 2, :], xstr[:, s, k, :],
                   meanb[:])
            if 1 <= k < KC + 1:
                xnf_done = op("vector", "tensor_mul", xnf[:, k - 1, :],
                              tmp2[:, (k - 1) % 2, :], rstdb[:], inc="VEC")
            if k >= 2:
                xn_done = op("vector", "tensor_copy",
                             xn_bf[:, k - 2, tt * 512:(tt + 1) * 512],
                             xnf[:, k - 2, :], inc="VEC")
        xstr_fr[s] = xn_done
        for tcl in range(4):
            tc = tt * 4 + tcl
            bk = 2 + (tcl % 2)
            for kc in range(KC):
                t = mm(bank(bk)[:, 0:8], xnf[:, kc, tcl * 128:(tcl + 1) * 128],
                       wrl[:, kc, :], kc == 0, kc == KC - 1,
                       waits=[xnf_done, wr_ld, bank_free[bk]])
            lg_done[tc] = op("vector", "tensor_copy", lgt[:, tc, :],
                             bank(bk)[:, 0:8], waits=[t], inc="VEC")
            bank_free[bk] = lg_done[tc]
    g1.close()

    g8 = contextlib.ExitStack()
    pbuf = g8.enter_context(nc.sbuf_tensor("pbuf", [128, NTC, 8], F32))
    prbuf = g8.enter_context(nc.sbuf_tensor("prbuf", [128, NTC, 8], F32))
    mk1b = g8.enter_context(nc.sbuf_tensor("mk1b", [128, NTC, 8], F32))
    mk2b = g8.enter_context(nc.sbuf_tensor("mk2b", [128, NTC, 8], F32))
    pmb = g8.enter_context(nc.sbuf_tensor("pmb", [128, NTC, 8], F32))

    def C(i):
        return scol[:, i:i + 1]

    def col(b, tc):
        return C(8 + b * NTC + tc)
    for tc in range(NTC):
        op("vector", "reduce_max", col(0, tc), lgt[:, tc, :], axis=AX.X,
           negate=True, waits=[lg_done[tc]])
    for tc in range(NTC):
        op("vector", "scalar_tensor_tensor", pbuf[:, tc, :], lgt[:, tc, :], 1.0,
           bc_ap(col(0, tc), lgt[:, tc, :]), ALU.mult, ALU.add)
    nm_mark = op("vector", "tensor_copy", C(0), C(0), inc="VEC")
    exp_done = None
    for tc in range(NTC):
        exp_done = op("scalar", "activation", pbuf[:, tc, :], pbuf[:, tc, :],
                      AF.Exp, bias=0.0, scale=1.0, waits=[nm_mark], inc="SCA")
    exp_mark = op("scalar", "copy", C(1), C(1), inc="SCA")
    for tc in range(NTC):
        op("vector", "reduce_sum", col(1, tc), pbuf[:, tc, :], axis=AX.X,
           waits=[exp_mark])
    for tc in range(NTC):
        op("vector", "reciprocal", col(2, tc), col(1, tc))
    for tc in range(NTC):
        op("vector", "tensor_mul", prbuf[:, tc, :], pbuf[:, tc, :],
           bc_ap(col(2, tc), pbuf[:, tc, :]))
    for tc in range(NTC):
        op("vector", "reduce_max", col(3, tc), prbuf[:, tc, :], axis=AX.X)
    for tc in range(NTC):
        op("vector", "tensor_tensor", mk1b[:, tc, :], prbuf[:, tc, :],
           bc_ap(col(3, tc), prbuf[:, tc, :]), ALU.is_ge)
    for tc in range(NTC):
        op("vector", "scalar_tensor_tensor", pmb[:, tc, :], mk1b[:, tc, :],
           -2.0, prbuf[:, tc, :], ALU.mult, ALU.add)
    for tc in range(NTC):
        op("vector", "reduce_max", col(4, tc), pmb[:, tc, :], axis=AX.X)
    for tc in range(NTC):
        op("vector", "tensor_tensor", mk2b[:, tc, :], pmb[:, tc, :],
           bc_ap(col(4, tc), pmb[:, tc, :]), ALU.is_ge)
    for tc in range(NTC):
        op("vector", "tensor_add", col(5, tc), col(3, tc), col(4, tc))
    for tc in range(NTC):
        op("vector", "reciprocal", col(6, tc), col(5, tc))
    for tc in range(NTC):
        op("vector", "tensor_mul", col(7, tc), col(3, tc), col(6, tc))
    for tc in range(NTC):
        op("vector", "tensor_mul", col(8, tc), col(4, tc), col(6, tc))
    for tc in range(NTC):
        op("vector", "tensor_mul", pbuf[:, tc, :], mk1b[:, tc, :],
           bc_ap(col(7, tc), mk1b[:, tc, :]))
    for tc in range(NTC):
        op("vector", "tensor_mul", pmb[:, tc, :], mk2b[:, tc, :],
           bc_ap(col(8, tc), mk2b[:, tc, :]))
    gate_done = None
    for tc in range(NTC):
        gate_done = op("vector", "tensor_add", gate[:, tc * 8:(tc + 1) * 8],
                       pbuf[:, tc, :], pmb[:, tc, :], inc="VEC")
    g8.close()
    if debug:
        dma("gpsimd", dbg["d_gate"], gate[:], waits=[gate_done])
        dma("gpsimd", dbg["d_xn"], xn_bf[:], waits=[xn_done])

    g2 = contextlib.ExitStack()
    dbc_f = g2.enter_context(nc.sbuf_tensor("dbc_f", [128, L], F32))
    xc_bf = g2.enter_context(nc.sbuf_tensor("xc_bf", [128, OC, L], F16))
    g2b = contextlib.ExitStack()
    xi_bf = g2b.enter_context(nc.sbuf_tensor("xi_bf", [128, 8, 3 + L], F16))
    cvtb = g2b.enter_context(nc.sbuf_tensor("cvtb", [128, 4, KC, 512], F16))

    ipw_r = ipw.rearrange("(j p) f -> p j f", p=128)
    CVS = ("CVa", "CVb", "CVc", "CVd")
    cvt_fr = [None] * 4
    last_xc_ev = None

    for g in range(2):
        pad_done = None
        for j in range(8):
            pad_done = op("vector", "memset", xi_bf[:, j, 0:3], 0.0, inc="VEC")
        xi_done = None
        for cb_i in range(2):
            c0 = g * 1024 + cb_i * 512
            s = cb_i % 2
            wcv = dma("gpsimd", cvtb[:, s], ipw_r[:, :, c0:c0 + 512],
                      waits=[cvt_fr[s], gate_done, exp_mark], sem=CVS[s])
            for ocb in range(4):
                oc_local = cb_i * 4 + ocb
                for tt in range(NTT):
                    bk = 4 + ((ocb * NTT + tt) % 4)
                    for j in range(KC):
                        t = mm(bk, cvtb[:, s, j, ocb * 128:(ocb + 1) * 128],
                               xn_bf[:, j, tt * 512:(tt + 1) * 512],
                               j == 0, j == KC - 1,
                               waits=[wcv, xn_done, pad_done, bank_free[bk]])
                    xi_done = op("scalar", "copy",
                                 xi_bf[:, oc_local, 3 + tt * 512:3 + (tt + 1) * 512],
                                 bank(bk), waits=[t], inc="SCA")
                    bank_free[bk] = xi_done
            cvt_fr[s] = t
        for ob in range(2):
            o0 = g * 1024 + ob * 512
            wcvs = []
            for w in range(4):
                wcv = dma("gpsimd", cvtb[:, w],
                          cw[w].rearrange("(j p) f -> p j f", p=128)[:, :, o0:o0 + 512],
                          waits=[cvt_fr[w]], sem=CVS[w])
                wcvs.append(wcv)
            for ocb in range(4):
                oc_g = g * 8 + ob * 4 + ocb
                for tt in range(NTT):
                    bk = 4 + ((ocb * NTT + tt) % 4)
                    first = True
                    for w in range(4):
                        for j in range(8):
                            t = mm(bk, cvtb[:, w, j, ocb * 128:(ocb + 1) * 128],
                                   xi_bf[:, j, w + tt * 512:w + (tt + 1) * 512],
                                   first, (w == 3 and j == 7),
                                   waits=[wcvs[w], xi_done, bank_free[bk]])
                            first = False
                    last_xc_ev = op("scalar", "activation",
                                    xc_bf[:, oc_g, tt * 512:(tt + 1) * 512],
                                    bank(bk), AF.Silu,
                                    bias=cbb[:, oc_g:oc_g + 1], scale=1.0,
                                    waits=[t, ld_small], inc="SCA")
                    bank_free[bk] = last_xc_ev
            for w in range(4):
                cvt_fr[w] = t
    xc_done = last_xc_ev
    p23_ten = ("TEN", cnt["TEN"])
    p23_vec = ("VEC", cnt["VEC"])
    g2b.close()

    xpl = g2.enter_context(nc.sbuf_tensor("xpl", [128, OC, 96], F32))
    xpb = g2.enter_context(nc.sbuf_tensor("xpb", [128, OC, 96], F16))
    r = dma("sync", xpl[:], xpw.rearrange("(j p) f -> p j f", p=128),
            waits=[p23_ten, p23_vec])
    xp_cv = op("vector", "tensor_copy", xpb[:], xpl[:], waits=[r], inc="VEC")
    for tt in range(NTT):
        bk = tt % 2
        for j in range(OC):
            t = mm(bank(bk)[0:96, :], xpb[:, j, :],
                   xc_bf[:, j, tt * 512:(tt + 1) * 512], j == 0, j == OC - 1,
                   waits=[xp_cv, xc_done, bank_free[bk]])
        ev = op("vector", "tensor_copy", dbc_f[0:96, tt * 512:(tt + 1) * 512],
                bank(bk)[0:96, :], waits=[t], inc="VEC")
        bank_free[bk] = ev
    dbc_cv = op("vector", "tensor_copy", dbc_b[0:96, :], dbc_f[0:96, :],
                inc="VEC")
    dbc_guard = op("vector", "tensor_copy", dbc_b[0:64, 0:512],
                   dbc_b[0:64, 0:512], inc="VEC")
    bc_rows = dma("gpsimd", sc_bc, dbc_b[64:96, :], waits=[dbc_guard])
    if debug:
        dma("gpsimd", dbg["d_dbc"], dbc_f[:], waits=[dbc_cv])

    dtl = g2.enter_context(nc.sbuf_tensor("dtl", [64, DI], F32))
    dtwb = g2.enter_context(nc.sbuf_tensor("dtwb", [64, DI], F16))
    sp1 = g2.enter_context(nc.sbuf_tensor("sp1", [128, 512], F32))
    sp2 = g2.enter_context(nc.sbuf_tensor("sp2", [128, 512], F32))
    dstage = g2.enter_context(nc.sbuf_tensor("dstage", [128, 3, 512], F16))
    r = dma("sync", dtl[:], dtw, waits=[xp_cv])
    dt_cv = op("vector", "tensor_copy", dtwb[:], dtl[:], waits=[r], inc="VEC")
    d_spill = [[None] * NTT for _ in range(OC)]
    xc_spill = [None] * OC
    dst_fr = [None, None, None]
    sp_fr = None
    pend = None
    for ocg in range(OC):
        for tt in range(NTT):
            bk = 2 + ((ocg * NTT + tt) % 2)
            t = mm(bk, dtwb[:, ocg * 128:(ocg + 1) * 128],
                   dbc_b[0:64, tt * 512:(tt + 1) * 512], True, True,
                   waits=[dt_cv, dbc_cv, bank_free[bk]])
            bc = dtbb[:, ocg:ocg + 1]
            op("scalar", "activation", sp1[:], bank(bk), AF.Abs, bias=bc,
               scale=1.0, waits=[t, sp_fr])
            op("scalar", "activation", sp1[:], sp1[:], AF.Exp, bias=0.0,
               scale=-1.0)
            op("scalar", "activation", sp1[:], sp1[:], AF.Ln, bias=1.0, scale=1.0)
            ab = op("scalar", "activation", sp2[:], bank(bk), AF.Relu, bias=bc,
                    scale=1.0, inc="SCA")
            bank_free[bk] = ab
            sl = (ocg * NTT + tt) % 3
            dv = op("vector", "tensor_add", dstage[:, sl, :], sp1[:], sp2[:],
                    waits=[ab, dst_fr[sl]], inc="VEC")
            sp_fr = dv
            if pend is not None:
                po, pt, psl = pend
                d_spill[po][pt] = dma("gpsimd",
                                      sc_delta[po][:, pt * 512:(pt + 1) * 512],
                                      dstage[:, psl, :], waits=[dv])
                dst_fr[psl] = d_spill[po][pt]
            pend = (ocg, tt, sl)
        xc_spill[ocg] = dma("gpsimd", sc_xc[ocg], xc_bf[:, ocg, :],
                            waits=[xc_done, ("SCA", cnt["SCA"])])
    guard = op("vector", "tensor_copy", sp1[:], sp1[:], inc="VEC")
    po, pt, psl = pend
    d_spill[po][pt] = dma("gpsimd", sc_delta[po][:, pt * 512:(pt + 1) * 512],
                          dstage[:, psl, :], waits=[guard])
    xn_spill = dma("gpsimd", sc_xn, xn_bf[:], waits=[xn_done, guard])
    gq_spills = ("GQ", cnt["GQ"])
    p45_ten = ("TEN", cnt["TEN"])
    p45_vec = ("VEC", cnt["VEC"])
    p45_sca = ("SCA", cnt["SCA"])
    g2.close()
    g_xn.close()
    if debug:
        dma("gpsimd", dbg["d_delta"], sc_delta, waits=[gq_spills])
        dma("gpsimd", dbg["d_xc"], sc_xc, waits=[gq_spills])

    g_yac = contextlib.ExitStack()
    yac = g_yac.enter_context(nc.sbuf_tensor("yac", [128, OC, L], F16))
    g3 = contextlib.ExitStack()
    bbn = g3.enter_context(nc.sbuf_tensor("bbn", [128, 2, L], F16))
    ccn = g3.enter_context(nc.sbuf_tensor("ccn", [128, 2, L], F16))
    dAb = g3.enter_context(nc.sbuf_tensor("dAb", [128, 3, L], F16))
    dBub = g3.enter_context(nc.sbuf_tensor("dBub", [128, 2, L], F16))
    hbf = g3.enter_context(nc.sbuf_tensor("hbf", [128, L], F16))
    ytm = g3.enter_context(nc.sbuf_tensor("ytm", [128, L], F16))

    bc_ev = [None, None]
    bc_used = [None, None]
    half_sca = None

    def build_bc(n, slot, extra=()):
        b1, _ = bass.broadcast_tensor_aps(sc_bc[n:n + 1, :], bbn[:, slot, :])
        c1, _ = bass.broadcast_tensor_aps(sc_bc[16 + n:17 + n, :], ccn[:, slot, :])
        dma("gpsimd", bbn[:, slot, :], b1,
            waits=[bc_rows, bc_used[slot]] + list(extra))
        bc_ev[slot] = dma("gpsimd", ccn[:, slot, :], c1, waits=[])

    half_ev = None
    for dh in range(NDH):
        g3h = contextlib.ExitStack()
        dlt = g3h.enter_context(nc.sbuf_tensor(f"dlt{dh}", [128, DCH, L], F16))
        xcs = g3h.enter_context(nc.sbuf_tensor(f"xcs{dh}", [128, DCH, L], F16))
        dub = xcs
        for i in range(DCH):
            dc = dh * DCH + i
            dma("sync", dlt[:, i, :], sc_delta[dc],
                waits=[gq_spills, half_ev, half_sca, p45_vec, p45_sca, p45_ten])
            dma("sync", xcs[:, i, :], sc_xc[dc], waits=[])
        all_loads = ("SQ", cnt["SQ"])
        du_ev = [None] * DCH
        drd = [all_loads] * DCH
        for i in range(DCH):
            dc = dh * DCH + i
            op("vector", "tensor_scalar", yac[:, dc, :], xcs[:, i, :],
               dvb[:, dc:dc + 1], None, ALU.mult, waits=[all_loads])
            du_ev[i] = op("vector", "tensor_mul", xcs[:, i, :], dlt[:, i, :],
                          xcs[:, i, :], inc="VEC")
        if dh == 0:
            build_bc(0, 0, extra=[dbc_cv, ("VEC", cnt["VEC"])])
        else:
            build_bc(0, 0)

        NSL = NS * DCH
        da_ev = [None] * (NSL + 4)
        scan_ev = [None] * NSL

        def emit_da(k):
            if k >= NSL + 2:
                return
            if k >= NSL:
                da_ev[k] = op("scalar", "copy", dAb[:, k % 3, :],
                              dAb[:, k % 3, :],
                              waits=[scan_ev[k - 3] if k >= 3 else None],
                              inc="SCA")
                return
            n, i_loc = divmod(k, DCH)
            dc = dh * DCH + i_loc
            if vecda:
                da_ev[k] = op("vector", "tensor_scalar_mul", dAb[:, k % 3, :],
                              dlt[:, i_loc, :], 0.1,
                              waits=[drd[i_loc], a_neg,
                                     scan_ev[k - 3] if k >= 3 else None],
                              inc="SCA")
            elif novecda:
                da_ev[k] = op("scalar", "mul", dAb[:, k % 3, :],
                              dlt[:, i_loc, :], 0.1,
                              waits=[drd[i_loc], a_neg,
                                     scan_ev[k - 3] if k >= 3 else None],
                              inc="SCA")
            else:
                da_ev[k] = op("scalar", "activation", dAb[:, k % 3, :],
                              dlt[:, i_loc, :], AF.Exp, bias=0.0,
                              scale=A_sb[:, dc * 16 + n:dc * 16 + n + 1],
                              waits=[drd[i_loc], a_neg,
                                     scan_ev[k - 3] if k >= 3 else None],
                              inc="SCA")

        emit_da(0)
        emit_da(1)
        op("vector", "tensor_mul", dBub[:, 0, :], dub[:, 0, :], bbn[:, 0, :],
           waits=[bc_ev[0], du_ev[DCH - 1]])
        for k in range(NSL):
            n, i_loc = divmod(k, DCH)
            dc = dh * DCH + i_loc
            ns = n % 2
            emit_da(k + 2)
            if i_loc == 1 and n + 1 < NS:
                build_bc(n + 1, (n + 1) % 2)
            if noscan:
                scan_ev[k] = op("vector", "tensor_mul", hbf[:],
                                dAb[:, k % 3, :], dBub[:, k % 2, :],
                                waits=[da_ev[k + 2]], inc="VEC")
            else:
                scan_ev[k] = op("vector", "tensor_tensor_scan", hbf[:],
                                dAb[:, k % 3, :], dBub[:, k % 2, :], 0.0,
                                ALU.mult, ALU.add, waits=[da_ev[k + 2]],
                                inc="VEC")
            if k + 1 < NSL:
                n2, i2 = divmod(k + 1, DCH)
                op("vector", "tensor_mul", dBub[:, (k + 1) % 2, :],
                   dub[:, i2, :], bbn[:, n2 % 2, :],
                   waits=[bc_ev[n2 % 2]])
            if ytmp_sep:
                op("vector", "tensor_mul", ytm[:], hbf[:], ccn[:, ns, :])
                op("vector", "tensor_add", yac[:, dc, :], yac[:, dc, :], ytm[:],
                   inc="VEC")
            else:
                op("vector", "tensor_mul", hbf[:], hbf[:], ccn[:, ns, :])
                op("vector", "tensor_add", yac[:, dc, :], yac[:, dc, :], hbf[:],
                   inc="VEC")
            if i_loc == 0:
                bc_used[(n + 1) % 2] = ("VEC", cnt["VEC"])
        half_ev = ("VEC", cnt["VEC"])
        half_sca = ("SCA", cnt["SCA"])
        bc_used[0] = bc_used[1] = half_ev
        g3h.close()
    scan_vec = ("VEC", cnt["VEC"])
    scan_sca = ("SCA", cnt["SCA"])
    scan_ten = ("TEN", cnt["TEN"])
    g3.close()

    g5 = contextlib.ExitStack()
    xn2 = g5.enter_context(nc.sbuf_tensor("xn2", [128, KC, L], F16))
    opw_b = g5.enter_context(nc.sbuf_tensor("opw_b", [128, OC, D], F16))
    rland = g5.enter_context(nc.sbuf_tensor("rland", [128, 2, KC, 128], F32))
    rbf = g5.enter_context(nc.sbuf_tensor("rbf", [128, 2, KC, 128], F16))
    ressl = g5.enter_context(nc.sbuf_tensor("ressl", [128, 2, L], F16))
    xch = g5.enter_context(nc.sbuf_tensor("xch", [128, 2, D], F32))

    xn_rd = dma("gpsimd", xn2[:], sc_xn,
                waits=[xn_spill, scan_vec, scan_sca, scan_ten])
    with nc.sbuf_tensor("oland", [128, 4, D], F32) as oland:
        opw_r = opw.rearrange("(c p) d -> p c d", p=128)
        opcv = None
        for blk in range(4):
            r = dma("sync", oland[:], opw_r[:, blk * 4:(blk + 1) * 4, :],
                    waits=[opcv, scan_vec])
            opcv = op("vector", "tensor_copy", opw_b[:, blk * 4:(blk + 1) * 4, :],
                      oland[:], waits=[r], inc="VEC")
        opw_done = opcv

    res_fr = [None, None]
    res_used = [None, None]
    rsl_ev = [None] * (OC + 1)
    ys_ev = [None] * OC

    def emit_res(dc):
        if dc >= OC:
            if dc == OC:
                rsl_ev[OC] = op("scalar", "copy", ressl[:, (OC - 1) % 2, 0:512],
                                ressl[:, (OC - 1) % 2, 0:512], inc="SCA")
            return
        rs = dc % 2
        rr = dma("sync", rland[:, rs],
                 ipw_r[:, :, DI + dc * 128:DI + (dc + 1) * 128],
                 waits=[res_fr[rs]], sem=("RQa", "RQb")[rs])
        rcv = op("vector", "tensor_copy", rbf[:, rs], rland[:, rs],
                 waits=[rr, res_used[rs]], inc="VEC")
        res_fr[rs] = rcv
        bk = 2 + rs
        for q in range(NTT):
            for j in range(KC):
                t = mm(bk, rbf[:, rs, j, :],
                       xn2[:, j, q * 512:(q + 1) * 512], j == 0, j == KC - 1,
                       waits=[rcv, xn_rd, bank_free[bk]])
            rsl_ev[dc] = op("scalar", "activation",
                            ressl[:, rs, q * 512:(q + 1) * 512], bank(bk),
                            AF.Silu, bias=0.0, scale=1.0, waits=[t], inc="SCA")
            bank_free[bk] = rsl_ev[dc]
        res_used[rs] = t

    emit_res(0)
    for dc in range(OC):
        emit_res(dc + 1)
        ys_ev[dc] = op("vector", "tensor_mul", yac[:, dc, :], yac[:, dc, :],
                       ressl[:, dc % 2, :], waits=[rsl_ev[dc + 1]], inc="VEC")
    ys_done = ys_ev[OC - 1]

    mamba_flush = [None] * NTC
    pend_fl = None
    add_ev = None
    for tc in range(NTC):
        xs = tc % 2
        prev_gq = ("GQ", cnt["GQ"]) if tc >= 2 else None
        xr = dma("sync", xch[:, xs, :], x_tm[tc * 128:(tc + 1) * 128, :],
                 waits=[prev_gq], sem=("XCa", "XCb")[xs])
        for dh2 in range(2):
            bk = 4 + ((tc * 2 + dh2) % 4)
            for ocg in range(OC):
                t = mm(bk, yac[:, ocg, tc * 128:(tc + 1) * 128],
                       opw_b[:, ocg, dh2 * 512:(dh2 + 1) * 512],
                       ocg == 0, ocg == OC - 1,
                       waits=[ys_done, opw_done, bank_free[bk]])
            add_ev = op("vector", "tensor_add",
                        xch[:, xs, dh2 * 512:(dh2 + 1) * 512],
                        xch[:, xs, dh2 * 512:(dh2 + 1) * 512], bank(bk),
                        waits=[t, xr], inc="VEC")
            bank_free[bk] = add_ev
            if pend_fl is not None and dh2 == 0:
                ptc, pxs = pend_fl
                mamba_flush[ptc] = dma("gpsimd", sc_mamba[ptc], xch[:, pxs, :],
                                       waits=[add_ev])
                pend_fl = None
        pend_fl = (tc, xs)
    fin_guard = op("vector", "tensor_copy", xch[:, (NTC - 1) % 2, 0:512],
                   xch[:, (NTC - 1) % 2, 0:512], inc="VEC")
    ptc, pxs = pend_fl
    mamba_flush[ptc] = dma("gpsimd", sc_mamba[ptc], xch[:, pxs, :],
                           waits=[fin_guard])
    p7_ten = ("TEN", cnt["TEN"])
    p7_vec = ("VEC", cnt["VEC"])
    p7_sca = ("SCA", cnt["SCA"])
    if debug:
        dma("gpsimd", dbg["d_mamba"], sc_mamba, waits=[("GQ", cnt["GQ"])])
        dma("gpsimd", dbg["d_ys"], yac[:], waits=[fin_guard])
    g5.close()
    g_yac.close()

    g4 = contextlib.ExitStack()
    xn3 = g4.enter_context(nc.sbuf_tensor("xn3", [128, KC, L], F16))
    h_bf = g4.enter_context(nc.sbuf_tensor("h_bf", [128, FCN, 1024], F16))
    w1b = g4.enter_context(nc.sbuf_tensor("w1b", [128, 2, KC, 512], F16))
    w2dh = g4.enter_context(nc.sbuf_tensor("w2dh", [128, 2, FCN // 2, 512], F16))
    out_h = g4.enter_context(nc.sbuf_tensor("out_h", [128, 8, D], F32))

    xn_rd2 = dma("gpsimd", xn3[:], sc_xn,
                 waits=[p7_ten, p7_vec, p7_sca, ("GQ", cnt["GQ"])])
    w1b_fr = [None, None]
    w2_fr = [None, None]
    w2_rd = [None, None]
    h_free = None
    final_flush = None
    ev = None
    w2_r = w2t.rearrange("e (f p) d -> e p f d", p=128)

    for th in range(2):
        init_r = []
        for tci in range(8):
            tc = th * 8 + tci
            init_r.append(dma("gpsimd", out_h[:, tci, :], sc_mamba[tc],
                              waits=[("GQ", cnt["GQ"]), p7_vec, p7_sca]))
        for e in range(NE):
            w1e = w1t[e].rearrange("(j p) f -> p j f", p=128)
            h_done = None

            def w2q(qi):
                dh2, fh = divmod(qi, 2)
                sl = qi % 2
                w2_rd[sl] = dma("gpsimd", w2dh[:, sl],
                                w2_r[e][:, fh * 16:(fh + 1) * 16,
                                        dh2 * 512:(dh2 + 1) * 512],
                                waits=[w2_fr[sl]], sem=("W2a", "W2b")[sl])
            wcv_pre = [dma("gpsimd", w1b[:, fb], w1e[:, :, fb * 512:(fb + 1) * 512],
                           waits=[w1b_fr[fb]], sem=("WQa", "WQb")[fb])
                       for fb in range(2)]
            w2q(0)
            w2q(1)
            for fb in range(8):
                s = fb % 2
                if fb < 2:
                    wcv = wcv_pre[fb]
                else:
                    wcv = dma("gpsimd", w1b[:, s],
                              w1e[:, :, fb * 512:(fb + 1) * 512],
                              waits=[w1b_fr[s]], sem=("WQa", "WQb")[s])
                for fcb in range(4):
                    fc = fb * 4 + fcb
                    for tt in range(2):
                        bk = (fcb * 2 + tt) % 4
                        for j in range(KC):
                            t = mm(bk, w1b[:, s, j, fcb * 128:(fcb + 1) * 128],
                                   xn3[:, j, th * 1024 + tt * 512:
                                       th * 1024 + (tt + 1) * 512],
                                   j == 0, j == KC - 1,
                                   waits=[wcv, xn_rd2, h_free, bank_free[bk]])
                        if fcb == 3 and tt == 1:
                            w1b_fr[s] = t
                        h_done = op("scalar", "activation",
                                    h_bf[:, fc, tt * 512:(tt + 1) * 512],
                                    bank(bk), AF.Silu, bias=0.0, scale=1.0,
                                    waits=[t], inc="SCA")
                        bank_free[bk] = h_done

            for dh2 in range(2):
                t_grp = None
                for fc in range(FCN):
                    fh, fl = divmod(fc, 16)
                    qi = dh2 * 2 + fh
                    sl = qi % 2
                    if fl == 0 and 1 <= qi < 3:
                        w2q(qi + 1)
                    for tci in range(8):
                        t_grp = mm(tci, h_bf[:, fc, tci * 128:(tci + 1) * 128],
                                   w2dh[:, sl, fl, :], fc == 0, fc == FCN - 1,
                                   waits=[w2_rd[sl], h_done, bank_free[tci]],
                                   inc_anyway=(tci == 7 and fc % 8 == 7))
                    if fl == 15:
                        w2_fr[sl] = t_grp
                for tci in range(8):
                    tc = th * 8 + tci
                    gcol = gate[:, tc * 8 + e:tc * 8 + e + 1]
                    ev = op("vector", "scalar_tensor_tensor",
                            out_h[:, tci, dh2 * 512:(dh2 + 1) * 512],
                            bank(tci), gcol,
                            out_h[:, tci, dh2 * 512:(dh2 + 1) * 512],
                            ALU.mult, ALU.add,
                            waits=[t_grp, gate_done, init_r[tci]], inc="VEC")
                    bank_free[tci] = ev
                h_free = t_grp

        flg = op("vector", "tensor_copy", h_bf[:, 0, :], h_bf[:, 0, :],
                 waits=[ev], inc="VEC")
        for tci in range(8):
            tc = th * 8 + tci
            final_flush = dma("gpsimd", out[tc * 128:(tc + 1) * 128, :],
                              out_h[:, tci, :], waits=[flg])

    g4.close()

    def fin(e_, s_=sems["GQ"], v=final_flush[1]):
        e_.wait_ge(s_, v)
    ops["gpsimd"].append(fin)

    with nc.Block() as block:
        @block.tensor
        def _(e):
            for f in ops["tensor"]:
                f(e)

        @block.vector
        def _(e):
            for f in ops["vector"]:
                f(e)

        @block.scalar
        def _(e):
            for f in ops["scalar"]:
                f(e)

        @block.gpsimd
        def _(e):
            for f in ops["gpsimd"]:
                f(e)

        @block.sync
        def _(e):
            for f in ops["sync"]:
                f(e)

    es.close()
    return nc


_NC_CACHE = {}


def _get_nc(debug=False):
    if debug not in _NC_CACHE:
        _NC_CACHE[debug] = build(debug)
    return _NC_CACHE[debug]


def make_in_maps(x, ipw, cw, cbv, xpw, dtwv, dtbv, alogv, dvecv, opwv, W1, W2, Wr):
    shared = {
        "ipw": np.ascontiguousarray(ipw, dtype=np.float32),
        "cw": np.ascontiguousarray(cw, dtype=np.float32),
        "cb": np.ascontiguousarray(np.asarray(cbv, np.float32).reshape(16, 128).T),
        "xpw": np.ascontiguousarray(xpw, dtype=np.float32),
        "dtw": np.ascontiguousarray(dtwv, dtype=np.float32),
        "dtb": np.ascontiguousarray(np.asarray(dtbv, np.float32).reshape(16, 128).T),
        "alog": np.ascontiguousarray(alogv, dtype=np.float32),
        "dvec": np.ascontiguousarray(np.asarray(dvecv, np.float32).reshape(16, 128).T),
        "opw": np.ascontiguousarray(opwv, dtype=np.float32),
        "w1t": np.ascontiguousarray(np.asarray(W1, np.float32).transpose(0, 2, 1)),
        "w2t": np.ascontiguousarray(np.asarray(W2, np.float32).transpose(0, 2, 1)),
        "wrt": np.ascontiguousarray(np.asarray(Wr, np.float32).T),
    }
    in_maps = []
    for c in range(x.shape[0]):
        m = dict(shared)
        m["x_tm"] = np.ascontiguousarray(x[c])
        m["x_fm"] = np.ascontiguousarray(x[c].T)
        in_maps.append(m)
    return in_maps


def kernel(L0_attn_x, L0_attn_in_proj_w, L0_attn_conv_w, L0_attn_conv_b,
           L0_attn_x_proj_w, L0_attn_dt_proj_w, L0_attn_dt_proj_b,
           L0_attn_A_log, L0_attn_D, L0_attn_out_proj_w,
           L0_ffn_W1, L0_ffn_W2, L0_ffn_Wr):
    x = np.ascontiguousarray(np.asarray(L0_attn_x, dtype=np.float32))
    in_maps = make_in_maps(x, L0_attn_in_proj_w, L0_attn_conv_w, L0_attn_conv_b,
                           L0_attn_x_proj_w, L0_attn_dt_proj_w, L0_attn_dt_proj_b,
                           L0_attn_A_log, L0_attn_D, L0_attn_out_proj_w,
                           L0_ffn_W1, L0_ffn_W2, L0_ffn_Wr)
    nc = _get_nc(False)
    res = run_bass_kernel_spmd(nc, in_maps, core_ids=list(range(8)))
    return np.stack([res.results[c]["out"] for c in range(8)], axis=0)
